# version 27
# baseline (speedup 1.0000x reference)
import bisect
import contextlib
import os
import numpy as np
import ml_dtypes
import concourse.bass as bass
import concourse.mybir as mybir
from concourse.bass_utils import run_bass_kernel_spmd

F32 = mybir.dt.float32
BF16 = mybir.dt.bfloat16
F16 = mybir.dt.float16
U8 = mybir.dt.uint8
A = mybir.AluOpType
AF = mybir.ActivationFunctionType
AX = mybir.AxisListType.X

H = 1024
DK = 1024
DV = 2048
LR = 16
IF = 2816
NIT = IF // 128
EPS = 1e-5
LN16 = float(np.log(16.0))
CH = 128
QMID = (0.25, 0.75, 1.25, 1.75, 2.5, 3.5, 5.0)

ENGS = ("tensor", "vector", "scalar", "gpsimd", "sync")
NDMA = 5
FULL_DRAINS = (os.environ.get("BASS_SIM_FP64") == "1"
               or os.environ.get("BASS_FORCE_DRAINS") == "1")
SMALL_KEYS = frozenset(
    ["qmx", "qsc", "qrsc", "dec", "t1", "cst", "ones", "gnw", "bgk",
     "lnS0", "lnS1", "lnQ0", "lnQ1", "lnM", "lnR", "ks",
     "lnM0", "lnM1", "lnR0", "lnR1",
     "msA0", "msA1", "msA2", "msA3", "msB", "msC", "p2M", "p2R"]
    + ["sig%d" % i for i in range(8)] + ["cs%d" % i for i in range(8)]
    + ["p2S%d" % i for i in range(16)] + ["p2Q%d" % i for i in range(16)])


def _ksmall(k):
    return k.split("@")[0] in SMALL_KEYS


class Prog:

    def __init__(self, nc, sems):
        self.nc = nc
        self.sems = sems
        self.cnt = {n: 0 for n in sems}
        self.ops = {e: [] for e in ENGS}
        self.last_wait = {}
        self.dma_idx = {"dg": 0, "ds": 0, "dc": 0}
        self.know = {e: {} for e in ENGS}
        self.semrec = {n: ([], []) for n in sems}
        self.nops = {e: 0 for e in ENGS}
        self.hist = {e: [] for e in ENGS}
        self.ndrains = {e: 0 for e in ENGS}

    @staticmethod
    def _merge(dst, src):
        for k, v in src.items():
            if dst.get(k, 0) < v:
                dst[k] = v

    def op(self, eng, fn, waits=(), inc=None, inc_amt=1, drain=None,
           rd=None, wr=None):
        know = self.know[eng]
        ws = []
        for w in waits:
            if w is None:
                continue
            n, v = w
            if v <= 0:
                continue
            vals, cums = self.semrec[n]
            i = bisect.bisect_right(vals, v) - 1
            if i >= 0:
                self._merge(know, cums[i])
            key = (eng, n)
            if self.last_wait.get(key, -1) >= v:
                continue
            self.last_wait[key] = v
            ws.append((self.sems[n], v))
        idx = self.nops[eng]
        pre_drain = False
        if drain is None and eng in ("vector", "scalar", "gpsimd"):
            floor = 0 if FULL_DRAINS else know.get(eng, 0)
            h = [t for t in self.hist[eng] if t[0] >= floor]
            if FULL_DRAINS or (rd is None and wr is None):
                pre_drain = bool(h)
                rdl, wrl, ann, big = ("*",), ("*",), False, False
            else:
                rdl, wrl, ann = tuple(rd or ()), tuple(wr or ()), True
                rds, wrs = set(rdl), set(wrl)
                big = any(not _ksmall(k) for k in rdl + wrl)
                nbig = sum(1 for t in h if t[4])
                seen_big = 0
                for (j, jr, jw, jann, jbig) in h:
                    if not jann:
                        pre_drain = True
                        break
                    jws = set(jw)
                    conf = (rds & jws) | (wrs & jws) | (wrs & set(jr))
                    if (conf and any(_ksmall(k) for k in conf)
                            and nbig - seen_big - (1 if jbig else 0) < 2):
                        pre_drain = True
                        break
                    if jbig:
                        seen_big += 1
            if pre_drain:
                know[eng] = idx
                h = []
                self.ndrains[eng] += 1
            self.hist[eng] = h + [(idx, rdl, wrl, ann, big)]
        self.nops[eng] = idx + 1
        if inc is not None:
            self.cnt[inc] += inc_amt
            snap = dict(know)
            if snap.get(eng, 0) < idx + 1:
                snap[eng] = idx + 1
            vals, cums = self.semrec[inc]
            base = dict(cums[-1]) if cums else {}
            self._merge(base, snap)
            vals.append(self.cnt[inc])
            cums.append(base)
        sem = self.sems[inc] if inc is not None else None

        def runner(e, fn=fn, ws=ws, sem=sem, amt=inc_amt, pre=pre_drain):
            if pre:
                e.drain()
            for s, v in ws:
                e.wait_ge(s, v)
            ins = fn(e)
            if sem is not None:
                ins.then_inc(sem, amt)

        self.ops[eng].append(runner)
        return self.cnt[inc] if inc is not None else None

    def dma(self, q, eng, fn, waits=()):
        i = self.dma_idx[q]
        self.dma_idx[q] = i + 1
        slot = "%s%d" % (q, i % NDMA)
        ws = list(waits) + [(slot, self.cnt[slot])]
        self.op(eng, fn, waits=ws, inc=slot, inc_amt=16, drain=False)
        return (slot, self.cnt[slot])

    def dma_fence(self, q):
        return [("%s%d" % (q, i), self.cnt["%s%d" % (q, i)]) for i in range(NDMA)]

    def v(self, name):
        return self.cnt[name]


def _ap(t, offset, dims):
    return bass.AP(t, offset, dims)


def build_program(T=2048, dbg=False):
    SB = min(256, T)
    NSB = T // SB
    CPS = SB // CH
    NT = T // 128
    nc = bass.Bass("TRN2", target_bir_lowering=False)

    x_d = nc.dram_tensor("x", [T, H], F32, kind="ExternalInput").ap()
    wq_d = nc.dram_tensor("Wq", [H, DK], F32, kind="ExternalInput").ap()
    wk_d = nc.dram_tensor("Wk", [H, DK], F32, kind="ExternalInput").ap()
    wv_d = nc.dram_tensor("Wv", [H, DV], F32, kind="ExternalInput").ap()
    wg_d = nc.dram_tensor("Wg", [H, DV], F32, kind="ExternalInput").ap()
    wgk1_d = nc.dram_tensor("Wgk1", [H, LR], F32, kind="ExternalInput").ap()
    wgk2_d = nc.dram_tensor("Wgk2", [LR, DK], F32, kind="ExternalInput").ap()
    bgk2_d = nc.dram_tensor("bgk2l", [128, 8], F32, kind="ExternalInput").ap()
    gnw_d = nc.dram_tensor("gnwl", [128, 16], F32, kind="ExternalInput").ap()
    wo_d = nc.dram_tensor("Wo", [DV, H], F32, kind="ExternalInput").ap()
    wgate_d = nc.dram_tensor("w_gate", [IF, H], F32, kind="ExternalInput").ap()
    wup_d = nc.dram_tensor("w_up", [IF, H], F32, kind="ExternalInput").ap()
    wdown_d = nc.dram_tensor("w_down", [H, IF], F32, kind="ExternalInput").ap()
    mask_d = nc.dram_tensor("maskT", [128, 512], BF16, kind="ExternalInput").ap()
    y_d = nc.dram_tensor("y", [T, H], F32, kind="ExternalOutput").ap()
    x2_d = nc.dram_tensor("x2d", [T, H], BF16,
                          kind="ExternalOutput" if dbg else "Internal").ap()
    wqkb_d = nc.dram_tensor("wqkb", [2, 1024, 1024], BF16).ap()
    wgT_d = nc.dram_tensor("wgTd", [128, NIT, 8, 128], BF16).ap()
    wuT_d = nc.dram_tensor("wuTd", [128, NIT, 8, 128], BF16).ap()
    wdT_d = nc.dram_tensor("wdTd", [128, NIT, 8, 128], BF16).ap()

    sem_names = (["pe", "ve", "se", "ge"]
                 + ["%s%d" % (q, i) for q in ("dg", "ds", "dc")
                    for i in range(NDMA)])

    with contextlib.ExitStack() as stack:
        E = stack.enter_context
        sems = {n: E(nc.semaphore(n)) for n in sem_names}
        P = Prog(nc, sems)
        pb = [E(nc.psum_tensor("pb%d" % i, [128, 512], F32)) for i in range(8)]

        cst = E(nc.sbuf_tensor("cst", [128, 12], F32))
        gnw = E(nc.sbuf_tensor("gnw", [128, 16], F32))
        bgk2 = E(nc.sbuf_tensor("bgk2s", [128, 8], F32))
        SCR = 3072
        qa_t = E(nc.sbuf_tensor("qa", [128, SCR], BF16))
        qA_t = E(nc.sbuf_tensor("qA", [128, SCR], BF16))
        qB_t = E(nc.sbuf_tensor("qB", [128, SCR], BF16))
        qP_t = E(nc.sbuf_tensor("qP", [128, SCR], BF16))
        mx_t = E(nc.sbuf_tensor("mx", [128, SCR // 16], F32))
        st2 = E(nc.sbuf_tensor("st2", [128, 4 * 16], F32))
        sc_t = E(nc.sbuf_tensor("sc", [128, SCR // 16], F32))
        rsc_t = E(nc.sbuf_tensor("rsc", [128, SCR // 16], F32))

        def quantize_steps(off, F, src_ap, dst_ap, waits=(), dst_waits=()):
            (st_, so, ss) = src_ap
            (dt_, do, ds_) = dst_ap
            G = F // 16
            g0 = off // 16
            L = "@%d" % off
            ksrc, kdst = st_.name, dt_.name
            m = {}
            src2 = _ap(st_, so, [[ss, 128], [1, F]])
            src3 = _ap(st_, so, [[ss, 128], [16, G], [1, 16]])
            qa2 = qa_t[:, off:off + F]
            qa3 = _ap(qa_t, off, [[SCR, 128], [16, G], [1, 16]])
            qA2 = qA_t[:, off:off + F]
            qB2 = qB_t[:, off:off + F]
            qP2 = qP_t[:, off:off + F]
            qP3 = _ap(qP_t, off, [[SCR, 128], [16, G], [1, 16]])
            rsc_b = _ap(rsc_t, g0, [[SCR // 16, 128], [1, G], [0, 16]])
            sc_b = _ap(sc_t, g0, [[SCR // 16, 128], [1, G], [0, 16]])
            dst3 = _ap(dt_, do, [[ds_, 128], [16, G], [1, 16]])
            W = lambda w: list(w() if callable(w) else w)
            S = []
            S.append(lambda: m.__setitem__("vR", P.op(
                "vector", lambda e: e.tensor_reduce(
                    mx_t[:, g0:g0 + G], src3, AX, A.max,
                    apply_absolute_value=True),
                waits=W(waits), inc="ve", rd=[ksrc], wr=["qmx" + L])))
            S.append(lambda: P.op("vector", lambda e: e.tensor_scalar(
                sc_t[:, g0:g0 + G], mx_t[:, g0:g0 + G],
                (1.0 / 6.0) * (1.0 + 2.0 ** -44), 1e-12 * (1.0 + 2.0 ** -44),
                A.mult, A.max), inc="ve", rd=["qmx" + L], wr=["qsc" + L]))
            S.append(lambda: P.op("vector", lambda e: e.reciprocal(
                rsc_t[:, g0:g0 + G], sc_t[:, g0:g0 + G]),
                inc="ve", rd=["qsc" + L], wr=["qrsc" + L]))
            S.append(lambda: m.__setitem__("sA", P.op(
                "scalar", lambda e: e.activation(qa2, src2, AF.Abs),
                waits=W(waits) + [("ve", m["vR"])], inc="se",
                rd=[ksrc], wr=["qa" + L])))
            S.append(lambda: m.__setitem__("sG", P.op(
                "scalar", lambda e: e.activation(src2, src2, AF.Sign),
                waits=[("ve", m["vR"])] + W(dst_waits), inc="se",
                rd=[ksrc], wr=[ksrc])))
            S.append(lambda: m.__setitem__("v1", P.op(
                "vector", lambda e: e.tensor_tensor(qa3, qa3, rsc_b, A.mult),
                waits=[("se", m["sA"])], inc="ve",
                rd=["qa" + L, "qrsc" + L], wr=["qa" + L])))
            S.append(lambda: m.__setitem__("s1", P.op(
                "scalar", lambda e: e.activation(
                    qA2, qa2, AF.Sign, bias=cst[:, 1:2]),
                waits=[("ve", m["v1"])], inc="se",
                rd=["qa" + L, "cst"], wr=["qA" + L])))
            S.append(lambda: m.__setitem__("s2", P.op(
                "scalar", lambda e: e.activation(
                    qB2, qa2, AF.Sign, bias=cst[:, 2:3]), inc="se",
                rd=["qa" + L, "cst"], wr=["qB" + L])))
            S.append(lambda: m.__setitem__("v2", P.op(
                "vector", lambda e: e.tensor_tensor(qP2, qA2, qB2, A.add),
                waits=[("se", m["s2"])], inc="ve",
                rd=["qA" + L, "qB" + L], wr=["qP" + L])))
            S.append(lambda: m.__setitem__("s3", P.op(
                "scalar", lambda e: e.activation(
                    qA2, qa2, AF.Sign, bias=cst[:, 3:4]),
                waits=[("ve", m["v2"])], inc="se",
                rd=["qa" + L, "cst"], wr=["qA" + L])))
            S.append(lambda: m.__setitem__("s4", P.op(
                "scalar", lambda e: e.activation(
                    qB2, qa2, AF.Sign, bias=cst[:, 4:5]), inc="se",
                rd=["qa" + L, "cst"], wr=["qB" + L])))
            S.append(lambda: m.__setitem__("v3", P.op(
                "vector", lambda e: e.tensor_tensor(qA2, qA2, qB2, A.add),
                waits=[("se", m["s4"])], inc="ve",
                rd=["qA" + L, "qB" + L], wr=["qA" + L])))
            S.append(lambda: m.__setitem__("s5", P.op(
                "scalar", lambda e: e.activation(
                    qB2, qa2, AF.Sign, bias=cst[:, 5:6]),
                waits=[("ve", m["v3"])], inc="se",
                rd=["qa" + L, "cst"], wr=["qB" + L])))
            S.append(lambda: m.__setitem__("v4", P.op(
                "vector", lambda e: e.tensor_tensor(qP2, qP2, qA2, A.add),
                inc="ve", rd=["qP" + L, "qA" + L], wr=["qP" + L])))
            S.append(lambda: m.__setitem__("s6", P.op(
                "scalar", lambda e: e.activation(
                    qA2, qa2, AF.Sign, bias=cst[:, 6:7]),
                waits=[("ve", m["v4"])], inc="se",
                rd=["qa" + L, "cst"], wr=["qA" + L])))
            S.append(lambda: m.__setitem__("v5", P.op(
                "vector", lambda e: e.tensor_tensor(qB2, qB2, qA2, A.add),
                waits=[("se", m["s6"])], inc="ve",
                rd=["qA" + L, "qB" + L], wr=["qB" + L])))
            S.append(lambda: m.__setitem__("s7", P.op(
                "scalar", lambda e: e.activation(
                    qA2, qa2, AF.Sign, bias=cst[:, 7:8]),
                waits=[("ve", m["v5"])], inc="se",
                rd=["qa" + L, "cst"], wr=["qA" + L])))
            S.append(lambda: P.op("vector", lambda e: e.scalar_tensor_tensor(
                qP2, qB2, 2.0, qP2, A.mult, A.add), inc="ve",
                rd=["qB" + L, "qP" + L], wr=["qP" + L]))
            S.append(lambda: P.op("vector", lambda e: e.scalar_tensor_tensor(
                qP2, qA2, 4.0, qP2, A.mult, A.add),
                waits=[("se", m["s7"])], inc="ve",
                rd=["qA" + L, "qP" + L], wr=["qP" + L]))
            S.append(lambda: P.op("vector", lambda e: e.tensor_scalar(
                qP2, qP2, 12.0, 0.25, A.add, A.mult), inc="ve",
                rd=["qP" + L], wr=["qP" + L]))
            S.append(lambda: m.__setitem__("v9", P.op(
                "vector", lambda e: e.tensor_tensor(qP2, qP2, src2, A.mult),
                waits=[("se", m["sG"])], inc="ve",
                rd=["qP" + L, ksrc], wr=["qP" + L])))
            S.append(lambda: m.__setitem__("vq", P.op(
                "vector", lambda e: e.tensor_tensor(dst3, qP3, sc_b, A.mult),
                waits=W(dst_waits), inc="ve",
                rd=["qP" + L, "qsc" + L], wr=[kdst])))
            return S, m

        def quantize(off, F, src_ap, dst_ap, waits=(), dst_waits=()):
            steps, m = quantize_steps(off, F, src_ap, dst_ap, waits, dst_waits)
            for s in steps:
                s()
            return m

        def quantize_prep_steps(off, F, m):
            G = F // 16
            g0 = off // 16
            L = "@%d" % off
            qa2 = qa_t[:, off:off + F]
            qa3 = _ap(qa_t, off, [[SCR, 128], [16, G], [1, 16]])
            qA2 = qA_t[:, off:off + F]
            qA3 = _ap(qA_t, off, [[SCR, 128], [16, G], [1, 16]])
            qB2 = qB_t[:, off:off + F]
            qB3 = _ap(qB_t, off, [[SCR, 128], [16, G], [1, 16]])
            qP2 = qP_t[:, off:off + F]
            qP3 = _ap(qP_t, off, [[SCR, 128], [16, G], [1, 16]])
            rsc_b = _ap(rsc_t, g0, [[SCR // 16, 128], [1, G], [0, 16]])
            sc_b = _ap(sc_t, g0, [[SCR // 16, 128], [1, G], [0, 16]])
            S = []
            S.append(lambda: m.__setitem__("vR", P.op(
                "vector", lambda e: e.tensor_reduce(
                    mx_t[:, g0:g0 + G], qA3, AX, A.max,
                    apply_absolute_value=True),
                waits=[m["dw"]], inc="ve", rd=["qA" + L], wr=["qmx" + L])))
            S.append(lambda: P.op("vector", lambda e: e.tensor_scalar(
                sc_t[:, g0:g0 + G], mx_t[:, g0:g0 + G],
                (1.0 / 6.0) * (1.0 + 2.0 ** -44), 1e-12 * (1.0 + 2.0 ** -44),
                A.mult, A.max), inc="ve", rd=["qmx" + L], wr=["qsc" + L]))
            S.append(lambda: P.op("vector", lambda e: e.reciprocal(
                rsc_t[:, g0:g0 + G], sc_t[:, g0:g0 + G]),
                inc="ve", rd=["qsc" + L], wr=["qrsc" + L]))
            S.append(lambda: m.__setitem__("sA", P.op(
                "scalar", lambda e: e.activation(qa2, qA2, AF.Abs),
                waits=[m["dw"], ("ve", m["vR"])], inc="se",
                rd=["qA" + L], wr=["qa" + L])))
            S.append(lambda: m.__setitem__("sG", P.op(
                "scalar", lambda e: e.activation(qA2, qA2, AF.Sign),
                waits=[("ve", m["vR"])], inc="se",
                rd=["qA" + L], wr=["qA" + L])))
            S.append(lambda: m.__setitem__("v1", P.op(
                "vector", lambda e: e.tensor_tensor(qa3, qa3, rsc_b, A.mult),
                waits=[("se", m["sA"])], inc="ve",
                rd=["qa" + L, "qrsc" + L], wr=["qa" + L])))
            S.append(lambda: m.__setitem__("s1", P.op(
                "scalar", lambda e: e.activation(
                    qP2, qa2, AF.Sign, bias=cst[:, 1:2]),
                waits=[("ve", m["v1"])], inc="se",
                rd=["qa" + L, "cst"], wr=["qP" + L])))
            for j, (wj, opj) in enumerate(((1.0, "add"), (1.0, "add"),
                                           (1.0, "add"), (2.0, "stt"),
                                           (2.0, "stt"), (4.0, "stt")),
                                          start=2):
                def s_j(j=j):
                    m["s%d" % j] = P.op(
                        "scalar", lambda e: e.activation(
                            qB2, qa2, AF.Sign, bias=cst[:, j:j + 1]),
                        waits=[("ve", m.get("v%d" % (j - 1), m["v1"])),
                               m.get("dtprev")], inc="se",
                        rd=["qa" + L, "cst"], wr=["qB" + L])
                S.append(s_j)

                def v_j(j=j, wj=wj, opj=opj):
                    if opj == "add":
                        m["v%d" % j] = P.op(
                            "vector", lambda e: e.tensor_tensor(
                                qP2, qP2, qB2, A.add),
                            waits=[("se", m["s%d" % j])], inc="ve",
                            rd=["qP" + L, "qB" + L], wr=["qP" + L])
                    else:
                        m["v%d" % j] = P.op(
                            "vector", lambda e, wj=wj: e.scalar_tensor_tensor(
                                qP2, qB2, wj, qP2, A.mult, A.add),
                            waits=[("se", m["s%d" % j])], inc="ve",
                            rd=["qP" + L, "qB" + L], wr=["qP" + L])
                S.append(v_j)
            S.append(lambda: P.op("vector", lambda e: e.tensor_scalar(
                qP2, qP2, 12.0, 0.25, A.add, A.mult), inc="ve",
                rd=["qP" + L], wr=["qP" + L]))
            S.append(lambda: m.__setitem__("v9", P.op(
                "vector", lambda e: e.tensor_tensor(qP2, qP2, qA2, A.mult),
                waits=[("se", m["sG"])], inc="ve",
                rd=["qP" + L, "qA" + L], wr=["qP" + L])))
            S.append(lambda: m.__setitem__("vq", P.op(
                "vector", lambda e: e.tensor_tensor(qB3, qP3, sc_b, A.mult),
                waits=[m.get("dtprev")], inc="ve",
                rd=["qP" + L, "qsc" + L], wr=["qB" + L])))
            return S

        with contextlib.ExitStack() as astack:
            EA = astack.enter_context
            ones = EA(nc.sbuf_tensor("ones", [128, 128], BF16))
            maskT = EA(nc.sbuf_tensor("maskTs", [128, 512], BF16))
            S_sb = EA(nc.sbuf_tensor("S", [128, 8, 512], BF16))
            scrA = EA(nc.sbuf_tensor("scrA", [128, 1024], BF16))
            wv_sb = EA(nc.sbuf_tensor("wv", [128, 8, 2048], BF16))
            wg_sb = EA(nc.sbuf_tensor("wg", [128, 8, 2048], BF16))
            wo_sb = EA(nc.sbuf_tensor("wo", [128, 16, 1024], BF16))
            wgk1_sb = EA(nc.sbuf_tensor("wgk1", [128, 8, 16], BF16))
            wgk2_sb = EA(nc.sbuf_tensor("wgk2", [16, 1024], BF16))
            wqs = [EA(nc.sbuf_tensor("wqs%d" % i, [128, 8, 128], BF16))
                   for i in range(2)]
            xbf = EA(nc.sbuf_tensor("xbf", [128, CPS, 1024], BF16))
            ln1f = EA(nc.sbuf_tensor("ln1f", [128, CPS, 8, 128], BF16))
            ln1t = EA(nc.sbuf_tensor("ln1t", [128, 1024], BF16))
            q_sb = EA(nc.sbuf_tensor("qsb", [128, 8, SB], BF16))
            k_sb = EA(nc.sbuf_tensor("ksb", [128, 8, SB], BF16))
            kT_sb = EA(nc.sbuf_tensor("kT", [128, 8 * CPS, 128], BF16))
            v_sb = EA(nc.sbuf_tensor("vsb", [128, 2048], BF16))
            sg_sb = EA(nc.sbuf_tensor("sgsb", [128, CPS, 2048], BF16))
            sig_sb = EA(nc.sbuf_tensor("sig", [128, 8, 128], F16))
            cs_sb = EA(nc.sbuf_tensor("cs", [128, 8, 128], F32))
            e_scr = EA(nc.sbuf_tensor("escr", [128, 8, 128], F32))
            dec_sb = EA(nc.sbuf_tensor("dec", [128, CPS, 8], F32))
            lnst = EA(nc.sbuf_tensor("lnst", [128, 16], F32))
            ms_sb = EA(nc.sbuf_tensor("ms", [128, 12], F32))
            ks_sb = EA(nc.sbuf_tensor("ks", [128, 8, 128], BF16))
            t1_sb = EA(nc.sbuf_tensor("t1", [16, 128], BF16))
            sT_sb = EA(nc.sbuf_tensor("sT", [128, 512], BF16))
            og_tm = EA(nc.sbuf_tensor("ogtm", [128, 2048], BF16))
            og_fm = EA(nc.sbuf_tensor("ogfm", [128, 16, 128], BF16))
            wqfm = EA(nc.sbuf_tensor("wqfm", [128, 24, 128], BF16))

            P.op("vector", lambda e: e.memset(ones[:], 1.0), inc="ve",
                 rd=[], wr=["ones"])
            P.op("vector", lambda e: e.memset(cst[:, 0:1], -LN16), inc="ve",
                 rd=[], wr=["cst"])
            for j, m in enumerate(QMID):
                P.op("vector", lambda e, j=j, m=m: e.memset(
                    cst[:, 1 + j:2 + j], -m), inc="ve", rd=[], wr=["cst"])
            P.op("vector", lambda e: e.memset(cst[:, 8:9], EPS), inc="ve",
                 rd=[], wr=["cst"])
            P.op("vector", lambda e: e.memset(cst[:, 9:10], 1.2e-38), inc="ve",
                 rd=[], wr=["cst"])
            P.op("vector", lambda e: e.memset(S_sb[:, :, :], 0.0), inc="ve",
                 rd=[], wr=["S"])
            dmask = P.dma("ds", "sync", lambda e: e.dma_start(
                out=maskT[:], in_=mask_d[:, :]))
            dgnw = P.dma("ds", "sync", lambda e: e.dma_start(
                out=gnw[:], in_=gnw_d[:, :]))
            dbg_ = P.dma("ds", "sync", lambda e: e.dma_start(
                out=bgk2[:], in_=bgk2_d[:, :]))
            st = {
                "peV": 0, "vV": 0, "peS": 0, "vS": 0, "peO": 0, "peK": {},
                "vSU": 0, "vOG": 0, "dsOG": None, "peY": 0, "vX2": 0,
                "dcX2": None, "peB": {}, "vB": {}, "dsKT": None, "dsLN": None,
                "vKi": 0, "vt1": 0,
            }
            st["dxpre"] = {}
            for tau in range(CPS):
                st["dxpre"][tau] = P.dma(
                    "dg", "gpsimd", lambda e, tau=tau: e.dma_start(
                        out=xbf[:, tau, :], in_=x_d[tau * 128:(tau + 1) * 128, :]))
            dgK1 = P.dma("dg", "gpsimd", lambda e: e.dma_start(
                out=wgk1_sb[:, :, :],
                in_=_ap(wgk1_d.tensor, 0, [[16, 128], [128 * 16, 8], [1, 16]])))
            dW = P.dma("dg", "gpsimd", lambda e: e.dma_start(
                out=wgk2_sb[:, :], in_=_ap(wgk2_d.tensor, 0, [[1024, 16], [1, 1024]])))
            stg = {}
            for i in range(16):
                wi, rt = divmod(i, 8)
                off = (i % 3) * 1024
                src_d = (wq_d if wi == 0 else wk_d)
                din = P.dma("dg", "gpsimd", lambda e, src_d=src_d, rt=rt,
                            off=off: e.dma_start(
                    out=qA_t[:, off:off + 1024],
                    in_=src_d[rt * 128:(rt + 1) * 128, :]),
                    waits=[stg.get(i - 3)])
                stg[i] = P.dma("dc", "scalar", lambda e, wi=wi, rt=rt, off=off:
                               e.dma_start(
                    out=_ap(wqkb_d.tensor, wi * 1024 * 1024 + rt * 128 * 1024,
                            [[1024, 128], [1, 1024]]),
                    in_=qA_t[:, off:off + 1024]),
                    waits=[din])
            dgWV = P.dma("dg", "gpsimd", lambda e: e.dma_start(
                out=wv_sb[:, :, :],
                in_=_ap(wv_d.tensor, 0, [[2048, 128], [128 * 2048, 8], [1, 2048]])))
            dgWG = P.dma("dg", "gpsimd", lambda e: e.dma_start(
                out=wg_sb[:, :, :],
                in_=_ap(wg_d.tensor, 0, [[2048, 128], [128 * 2048, 8], [1, 2048]])))
            stg_done = [stg[13], stg[14], stg[15]]

            lane_prev = {
                0: {"v9": P.v("ve"), "sG": P.v("se"), "dw0": stg.get(15)},
                1024: {"v9": P.v("ve"), "sG": P.v("se"), "dw0": stg.get(13)},
                2048: {"v9": P.v("ve"), "sG": P.v("se"), "dw0": stg.get(14)}}

            def mk_prep(load_fn, F, out_fn):
                def factory(off):
                    prev = lane_prev[off]
                    nblk = F // 128
                    blk0 = 8 * (off // 1024)
                    qm = {"dtprev": prev.get("dt")}
                    steps = [lambda: qm.__setitem__("dw", P.dma(
                        "dg", "gpsimd", lambda e: e.dma_start(
                            out=qA_t[:, off:off + F], in_=load_fn()),
                        waits=[("ve", prev.get("v9", 0)),
                               ("se", prev.get("sG", 0)),
                               prev.get("dw0")]))]
                    steps += quantize_prep_steps(off, F, qm)
                    steps.append(lambda: qm.__setitem__("dt", P.dma(
                        "ds", "sync", lambda e: e.dma_start_transpose(
                            _ap(wqfm, blk0 * 128,
                                [[24 * 128, 128], [128, nblk], [1, 128]]),
                            qB_t[:, off:off + F]),
                        waits=[("ve", qm["vq"]), prev.get("st")])))
                    steps.append(lambda: qm.__setitem__("st", P.dma(
                        "dc", "scalar", lambda e: e.dma_start(
                            out=out_fn(),
                            in_=_ap(wqfm, blk0 * 128,
                                    [[24 * 128, 128], [128, nblk], [1, 128]])),
                        waits=[qm["dt"]])))
                    lane_prev[off] = qm
                    return steps
                prep_units.append(factory)

            prep_units = []
            wo_state = {"last": 0}

            def mk_wo(dv2):
                def factory(off):
                    prev = lane_prev[off]
                    u = {}

                    def wo_dma():
                        u["d"] = P.dma("dg", "gpsimd", lambda e: e.dma_start(
                            out=qA_t[:, off:off + 1024],
                            in_=_ap(wo_d.tensor, dv2 * 128 * 1024,
                                    [[1024, 128], [1, 1024]])),
                            waits=[("ve", prev.get("v9", 0)),
                                   ("se", prev.get("sG", 0)),
                                   prev.get("dw0"), prev.get("dt")])

                    def wo_ts():
                        m = P.op("vector", lambda e: e.tensor_scalar(
                            wo_sb[:, dv2, :], qA_t[:, off:off + 1024],
                            gnw[:, dv2:dv2 + 1], None, A.mult),
                            waits=[u["d"], dgnw], inc="ve",
                            rd=["qA@%d" % off, "gnw"], wr=["wo"])
                        wo_state["last"] = m
                        lane_prev[off] = {"v9": m, "sG": prev.get("sG", 0),
                                          "dw0": prev.get("dw0"),
                                          "dt": prev.get("dt")}
                    return [wo_dma, wo_ts]
                prep_units.append(factory)

            for dv2 in range(16):
                mk_wo(dv2)

            for w_src, w_dst in ((wgate_d, wgT_d), (wup_d, wuT_d)):
                for it in range(NIT):
                    mk_prep(
                        (lambda w_src=w_src, it=it: _ap(
                            w_src.tensor, it * 128 * 1024,
                            [[1024, 128], [1, 1024]])),
                        1024,
                        (lambda w_dst=w_dst, it=it: w_dst[:, it, :, :]))
            for ht in range(8):
                for sl0, F in ((0, 1024), (1024, 1024), (2048, 768)):
                    it0 = sl0 // 128
                    nblk = F // 128
                    mk_prep(
                        (lambda ht=ht, sl0=sl0, F=F: _ap(
                            wdown_d.tensor, ht * 128 * IF + sl0,
                            [[IF, 128], [1, F]])),
                        F,
                        (lambda ht=ht, it0=it0, nblk=nblk: _ap(
                            wdT_d.tensor, it0 * (8 * 128) + ht * 128,
                            [[NIT * 8 * 128, 128], [8 * 128, nblk], [1, 128]])))

            psched = {"active": {0: [], 1024: [], 2048: []}, "turn": [0],
                      "points": NSB * 17}
            NPREPSTEPS = (len(prep_units) - 16) * 25 + 16 * 2

            def prep_remaining():
                return (len(prep_units) * 24 + len(psched["active"][0])
                        + len(psched["active"][1024])
                        + len(psched["active"][2048]))

            def pump(k=1):
                pts = max(psched["points"], 1)
                psched["points"] -= k
                n = k * max(1, (prep_remaining() + pts - 1) // pts)
                for _ in range(n):
                    done = True
                    for _try in range(3):
                        off = 1024 * psched["turn"][0]
                        psched["turn"][0] = (psched["turn"][0] + 1) % 3
                        act = psched["active"][off]
                        if not act and prep_units:
                            act[:] = prep_units.pop(0)(off)
                        if act:
                            act.pop(0)()
                            done = False
                            break
                    if done:
                        return

            def prep_left():
                return (prep_units or psched["active"][0]
                        or psched["active"][1024] or psched["active"][2048])

            def a_steps(sb, tau):
                c = sb * CPS + tau
                u = {}
                steps = []

                def s_dx():
                    if sb == 0 and st["dxpre"].get(tau) is not None:
                        u["dx"] = st["dxpre"].pop(tau)
                    else:
                        u["dx"] = P.dma(
                            "dg", "gpsimd", lambda e: e.dma_start(
                                out=xbf[:, tau, :],
                                in_=x_d[c * 128:(c + 1) * 128, :]),
                            waits=[("ve", st.get("vX2t%d" % tau, 0)),
                                   st.get("dcX2t%d" % tau)])
                steps.append(s_dx)

                def s_red():
                    u["red"] = P.op("vector", lambda e: e.tensor_reduce(
                        lnst[:, tau:tau + 1], xbf[:, tau, :], AX, A.add),
                        waits=[u["dx"]], inc="ve",
                        rd=["xbf"], wr=["lnS%d" % tau])
                    u["ssq"] = P.op("scalar", lambda e: e.activation(
                        scrA[:, :], xbf[:, tau, :], AF.Square,
                        accum_out=lnst[:, 4 + tau:5 + tau]),
                        waits=[u["dx"]], inc="se",
                        rd=["xbf"], wr=["scrA", "lnQ%d" % tau])
                steps.append(s_red)

                def s_stats():
                    P.op("vector", lambda e: e.tensor_scalar(
                        lnst[:, tau:tau + 1], lnst[:, tau:tau + 1], 1.0 / H,
                        None, A.mult), inc="ve",
                        rd=["lnS%d" % tau], wr=["lnS%d" % tau])
                    P.op("vector", lambda e: e.tensor_scalar(
                        lnst[:, 4 + tau:5 + tau], lnst[:, 4 + tau:5 + tau],
                        1.0 / H, EPS, A.mult, A.add),
                        waits=[("se", u["ssq"])], inc="ve",
                        rd=["lnQ%d" % tau], wr=["lnQ%d" % tau])
                    P.op("vector", lambda e: e.scalar_tensor_tensor(
                        lnst[:, 8 + tau:9 + tau], lnst[:, tau:tau + 1], 1.0,
                        lnst[:, tau:tau + 1], A.mult, A.mult), inc="ve",
                        rd=["lnS%d" % tau], wr=["lnM%d" % tau])
                    u["var"] = P.op("vector", lambda e: e.tensor_tensor(
                        lnst[:, 4 + tau:5 + tau], lnst[:, 4 + tau:5 + tau],
                        lnst[:, 8 + tau:9 + tau], A.subtract), inc="ve",
                        rd=["lnQ%d" % tau, "lnM%d" % tau],
                        wr=["lnQ%d" % tau])
                steps.append(s_stats)

                def s_lnexp():
                    P.op("scalar", lambda e: e.activation(
                        lnst[:, 8 + tau:9 + tau], lnst[:, 4 + tau:5 + tau],
                        AF.Ln), waits=[("ve", u["var"])], inc="se",
                        rd=["lnQ%d" % tau], wr=["lnM%d" % tau])
                    u["sr"] = P.op("scalar", lambda e: e.activation(
                        lnst[:, 12 + tau:13 + tau], lnst[:, 8 + tau:9 + tau],
                        AF.Exp, scale=-0.5), inc="se",
                        rd=["lnM%d" % tau], wr=["lnR%d" % tau])
                steps.append(s_lnexp)

                def s_vl():
                    vl = P.op("vector", lambda e: e.tensor_scalar(
                        ln1t[:, :], xbf[:, tau, :], lnst[:, tau:tau + 1],
                        lnst[:, 12 + tau:13 + tau], A.subtract, A.mult),
                        waits=[("se", u["sr"]), st["dsLN"]], inc="ve",
                        rd=["xbf", "lnS%d" % tau, "lnR%d" % tau], wr=["ln1t"])
                    st["dsLN"] = P.dma("ds", "sync", lambda e:
                                       e.dma_start_transpose(
                        ln1f[:, tau, :, :], ln1t[:, :]),
                        waits=[("ve", vl), ("pe", st["peV"])])
                steps.append(s_vl)
                return steps

            def build_b(sb):
                b_steps = []
                for wi in range(2):
                    for oc in range(8):
                        slot = (wi * 8 + oc) % 2
                        bank = pb[slot]
                        key = (wi, oc)
                        dst = q_sb if wi == 0 else k_sb
                        u = {}

                        def bdma(wi=wi, oc=oc, slot=slot, u=u):
                            q_, e_ = (("dc", "scalar") if slot == 0
                                      else ("ds", "sync"))
                            u["d"] = P.dma(q_, e_, lambda e: e.dma_start(
                                out=wqs[slot][:, :, :],
                                in_=_ap(wqkb_d.tensor,
                                        wi * 1024 * 1024 + oc * 128,
                                        [[1024, 128], [128 * 1024, 8],
                                         [1, 128]])),
                                waits=[("pe", st.get("wqsPE%d" % slot, 0))]
                                + stg_done)

                        def bmm(wi=wi, oc=oc, slot=slot, bank=bank, key=key,
                                u=u):
                            prev_vb = st.get("wqsVB%d" % slot, 0)
                            for hc in range(8):
                                mov = _ap(ln1f, hc * 128,
                                          [[CPS * 8 * 128, 128],
                                           [8 * 128, CPS], [1, 128]])
                                st["peB"][key] = P.op(
                                    "tensor", lambda e, bank=bank, slot=slot,
                                    hc=hc, mov=mov: e.matmul(
                                        bank[:, 0:SB], wqs[slot][:, hc, :],
                                        mov,
                                        start=(hc == 0), stop=(hc == 7)),
                                    waits=[u["d"], st["dsLN"],
                                           ("se", prev_vb),
                                           ("ve", st["vOG"])],
                                    inc="pe" if hc == 7 else None)
                            st["wqsPE%d" % slot] = st["peB"][key]

                        def bev(wi=wi, oc=oc, slot=slot, bank=bank, key=key,
                                dst=dst):
                            st["vB"][key] = P.op(
                                "scalar", lambda e: e.activation(
                                    dst[:, oc, :], bank[:, 0:SB], AF.Copy),
                                waits=[("pe", st["peB"][key])], inc="se",
                                rd=["pb%d" % slot],
                                wr=["qsb" if wi == 0 else "ksb"])
                            st["wqsVB%d" % slot] = st["vB"][key]

                        b_steps += [bdma, bmm, bev]
                return b_steps

            sigk = ["sig%d" % i for i in range(8)]
            csk = ["cs%d" % i for i in range(8)]

            def build_c(sb):
                ctau = {0: [], 1: []}
                for tau in range(CPS):
                    cu = {}

                    def c_t1(tau=tau, cu=cu):
                        peT1 = None
                        for hc in range(8):
                            peT1 = P.op("tensor", lambda e, hc=hc, tau=tau:
                                        e.matmul(
                                pb[2][0:16, 0:128], wgk1_sb[:, hc, :],
                                ln1f[:, tau, hc, :],
                                start=(hc == 0), stop=(hc == 7)),
                                waits=[dW, dgK1, st["dsLN"],
                                       ("se", st["vt1"]),
                                       ("ve", st["vOG"])],
                                inc="pe" if hc == 7 else None)
                        cu["vt1"] = st["vt1"] = P.op(
                            "scalar", lambda e: e.activation(
                                t1_sb[0:16, :], pb[2][0:16, 0:128], AF.Copy),
                            waits=[("pe", peT1)], inc="se",
                            rd=["pb2"], wr=["t1"])
                    ctau[tau].append(c_t1)

                    for grp in range(2):
                        def c_gk(tau=tau, grp=grp, cu=cu):
                            pegk = None
                            for j in range(4):
                                oc = grp * 4 + j
                                pegk = P.op("tensor", lambda e, oc=oc, j=j:
                                            e.matmul(
                                    pb[3][:, j * 128:(j + 1) * 128],
                                    wgk2_sb[0:16, oc * 128:(oc + 1) * 128],
                                    t1_sb[0:16, :], start=True, stop=True),
                                    waits=[("se", cu["vt1"]),
                                           ("se", st.get("seC1v", 0))],
                                    inc="pe" if j == 3 else None)
                            for j in range(4):
                                oc = grp * 4 + j
                                st["seC1v"] = P.op(
                                    "scalar", lambda e, oc=oc, j=j:
                                    e.activation(
                                        sig_sb[:, oc, :],
                                        pb[3][:, j * 128:(j + 1) * 128],
                                        AF.Sigmoid, bias=bgk2[:, oc:oc + 1],
                                        scale=1.0),
                                    waits=[("pe", pegk), dbg_,
                                           ("se", st.get("vclp", 0))],
                                    inc="se", rd=["pb3", "bgk"],
                                    wr=["sig%d" % oc])
                        ctau[tau].append(c_gk)

                    def c_cl(tau=tau, cu=cu):
                        cu["sln"] = st["vclp"] = P.op(
                            "scalar", lambda e: e.activation(
                                e_scr[:, :, :], sig_sb[:, :, :], AF.Ln,
                                bias=cst[:, 9:10]),
                            waits=[("se", st["seC1v"]),
                                   ("ve", st.get("vKi", 0))],
                            inc="se", rd=sigk + ["cst"], wr=["escr"])
                    ctau[tau].append(c_cl)

                    for half in range(2):
                        def c_scan(tau=tau, half=half, cu=cu):
                            for oc in range(half * 4, half * 4 + 4):
                                cu["vscan"] = P.op(
                                    "vector", lambda e, oc=oc:
                                    e.tensor_tensor_scan(
                                        cs_sb[:, oc, :], ones[:, :],
                                        e_scr[:, oc, :], 0.0, A.mult, A.add),
                                    waits=[("se", cu["sln"]),
                                           ("se", st.get("seEmb", 0))],
                                    inc="ve", rd=["ones", "escr"],
                                    wr=["cs%d" % oc])
                        ctau[tau].append(c_scan)

                    def c_dec(tau=tau, cu=cu):
                        st["seDec"] = P.op(
                            "scalar", lambda e, tau=tau: e.activation(
                                dec_sb[:, tau, :],
                                _ap(cs_sb, 127, [[8 * 128, 128], [128, 8]]),
                                AF.Exp, scale=1.0 / 16),
                            waits=[("ve", cu["vscan"]), ("ve", st["vSU"])],
                            inc="se", rd=csk, wr=["dec"])
                        cu["sebq"] = P.op("scalar", lambda e: e.activation(
                            e_scr[:, :, :], cs_sb[:, :, :], AF.Exp,
                            scale=1.0 / 16, bias=cst[:, 0:1]),
                            waits=[("ve", cu["vscan"])], inc="se",
                            rd=csk + ["cst"], wr=["escr"])
                    ctau[tau].append(c_dec)

                    def c_qg(tau=tau, cu=cu):
                        vqg = P.op("vector", lambda e, tau=tau: e.tensor_tensor(
                            _ap(q_sb, tau * 128,
                                [[8 * SB, 128], [SB, 8], [1, 128]]),
                            _ap(q_sb, tau * 128,
                                [[8 * SB, 128], [SB, 8], [1, 128]]),
                            e_scr[:, :, :], A.mult),
                            waits=[("se", cu["sebq"]), ("pe", st["peO"]),
                                   ("se", st["vB"][(0, 7)])], inc="ve",
                            rd=["escr", "qsb"], wr=["qsb"])
                        st["seEmb"] = P.op("scalar", lambda e: e.activation(
                            e_scr[:, :, :], cs_sb[:, :, :], AF.Exp,
                            scale=-1.0 / 16),
                            waits=[("ve", vqg)], inc="se",
                            rd=csk, wr=["escr"])
                        st["vKi"] = P.op(
                            "vector", lambda e, tau=tau: e.tensor_tensor(
                                _ap(k_sb, tau * 128,
                                    [[8 * SB, 128], [SB, 8], [1, 128]]),
                                _ap(k_sb, tau * 128,
                                    [[8 * SB, 128], [SB, 8], [1, 128]]),
                                e_scr[:, :, :], A.mult),
                            waits=[("se", st["seEmb"]), ("pe", st["peO"]),
                                   ("se", st["vB"][(1, 7)])], inc="ve",
                            rd=["escr", "ksb"], wr=["ksb"])
                    ctau[tau].append(c_qg)

                    def c_ks(tau=tau, cu=cu):
                        vks = None
                        for oc in range(8):
                            vks = P.op("scalar", lambda e, oc=oc, tau=tau:
                                       e.activation(
                                ks_sb[:, oc, :],
                                k_sb[:, oc, tau * 128:(tau + 1) * 128],
                                AF.Copy, scale=dec_sb[:, tau, oc:oc + 1]),
                                waits=[("ve", st["vKi"]),
                                       st.get("dsKTlast")], inc="se",
                                rd=["ksb", "dec"], wr=["ks"])
                        st["dsKT%d" % tau] = st["dsKTlast"] = P.dma(
                            "ds", "sync", lambda e, tau=tau:
                            e.dma_start_transpose(
                                _ap(kT_sb, tau * 8 * 128,
                                    [[16 * 128, 128], [128, 8], [1, 128]]),
                                _ap(ks_sb, 0, [[8 * 128, 128], [1, 8 * 128]])),
                            waits=[("se", vks), ("pe", st["peK"].get(tau, 0))])
                    ctau[tau].append(c_ks)
                return ctau

            nxt = {}
            for sb in range(NSB):
                c0 = sb * CPS
                if sb == 0:
                    for s in a_steps(0, 0):
                        s()
                    ct = build_c(0)
                    pre = 0
                else:
                    ct = nxt["ct"]
                    pre = nxt["pre"]
                for s in a_steps(sb, 1):
                    s()
                pump(1)

                b_steps = nxt["b"] if sb else build_b(sb)
                eu = {}

                def e_steps(tau):
                    def e_mm(tau=tau):
                        prev_se = st.get("seE", 0)
                        peg = None
                        for hc in range(8):
                            for b4 in range(4):
                                peg = P.op("tensor", lambda e, hc=hc, b4=b4,
                                           tau=tau: e.matmul(
                                    pb[4 + b4][:, :], ln1f[:, tau, hc, :],
                                    wg_sb[:, hc, b4 * 512:(b4 + 1) * 512],
                                    start=(hc == 0), stop=(hc == 7)),
                                    waits=[dgWG, ("se", prev_se),
                                           ("ve", st["vX2"])],
                                    inc="pe" if (hc == 7 and b4 == 3)
                                    else None)
                        eu[tau] = peg

                    def e_si(tau=tau):
                        for b4 in range(4):
                            st["seE"] = P.op(
                                "scalar", lambda e, b4=b4, tau=tau:
                                e.activation(
                                    sg_sb[:, tau, b4 * 512:(b4 + 1) * 512],
                                    pb[4 + b4][:, :], AF.Silu),
                                waits=[("pe", eu[tau])], inc="se",
                                rd=["pb%d" % (4 + b4)], wr=["sg"])
                    return [e_mm, e_si]

                cpre = ct[0][pre:7] + ct[1][:3]
                ev = e_steps(0) + e_steps(1)
                mixed = []
                while cpre or ev:
                    if cpre:
                        mixed.append(cpre.pop(0))
                    if cpre:
                        mixed.append(cpre.pop(0))
                    if ev:
                        mixed.append(ev.pop(0))
                c_steps = mixed + ct[0][7:] + ct[1][3:]
                gate = len(mixed)
                bi = ci = 0
                while bi < len(b_steps) or ci < len(c_steps):
                    for _ in range(5):
                        if bi < len(b_steps):
                            b_steps[bi]()
                            bi += 1
                    if bi % 15 == 0 or bi >= len(b_steps):
                        pump(1)
                    if ci < len(c_steps) and not (ci >= gate
                                                  and bi < len(b_steps)):
                        c_steps[ci]()
                        ci += 1
                pump(1)

                for tau in range(CPS):
                    c = c0 + tau
                    pev = None
                    for hc in range(8):
                        for b4 in range(4):
                            pev = P.op("tensor", lambda e, hc=hc, b4=b4,
                                       tau=tau: e.matmul(
                                pb[b4][:, :], ln1f[:, tau, hc, :],
                                wv_sb[:, hc, b4 * 512:(b4 + 1) * 512],
                                start=(hc == 0), stop=(hc == 7)),
                                waits=[dgWV, ("ve", st["vOG"]),
                                       ("ve", st["vKi"]),
                                       ("se", st["seE"])],
                                inc="pe" if (hc == 7 and b4 == 3) else None)
                    st["peV"] = pev
                    vv = None
                    for b4 in range(4):
                        vv = P.op("scalar", lambda e, b4=b4: e.activation(
                            v_sb[:, b4 * 512:(b4 + 1) * 512], pb[b4][:, :],
                            AF.Copy),
                            waits=[("pe", pev)], inc="se",
                            rd=["pb%d" % b4], wr=["vsb"])
                    st["vV"] = vv
                    pes = None
                    for h in range(4):
                        for dc in range(2):
                            oc = 2 * h + dc
                            pes = P.op("tensor", lambda e, h=h, oc=oc, dc=dc,
                                       tau=tau: e.matmul(
                                pb[4][:, h * 128:(h + 1) * 128],
                                _ap(k_sb, oc * SB + tau * 128,
                                    [[8 * SB, 128], [1, 128]]),
                                _ap(q_sb, oc * SB + tau * 128,
                                    [[8 * SB, 128], [1, 128]]),
                                start=(dc == 0), stop=(dc == 1)),
                                waits=[("ve", st["vKi"]), ("ve", st["vX2"])],
                                inc="pe" if (h == 3 and dc == 1) else None)
                    st["peS"] = pes
                    vs = P.op("vector", lambda e: e.tensor_tensor(
                        sT_sb[:, :], pb[4][:, :], maskT[:, :], A.mult),
                        waits=[("pe", pes), ("pe", st["peO"]), dmask], inc="ve",
                        rd=["mask", "pb4"], wr=["sT"])
                    st["vS"] = vs
                    peo = None
                    for h in range(4):
                        peo = P.op("tensor", lambda e, h=h: e.matmul(
                            pb[h][:, :], sT_sb[:, h * 128:(h + 1) * 128],
                            v_sb[:, h * 512:(h + 1) * 512], start=True,
                            stop=False),
                            waits=[("ve", vs), ("se", vv), ("ve", st["vSU"])],
                            inc=None)
                        for dc in range(2):
                            oc = 2 * h + dc
                            peo = P.op("tensor", lambda e, h=h, oc=oc, dc=dc,
                                       tau=tau: e.matmul(
                                pb[h][:, :],
                                _ap(q_sb, oc * SB + tau * 128,
                                    [[8 * SB, 128], [1, 128]]),
                                S_sb[:, oc, :], start=False, stop=(dc == 1)),
                                inc="pe" if (h == 3 and dc == 1) else None)
                    st["peO"] = peo
                    vsu = None
                    for oc in range(8):
                        bank = pb[5 + oc % 2]
                        pek = P.op("tensor", lambda e, oc=oc, bank=bank,
                                   tau=tau: e.matmul(
                            bank[:, :], kT_sb[:, tau * 8 + oc, :],
                            v_sb[:, (oc // 2) * 512:(oc // 2 + 1) * 512],
                            start=True, stop=True),
                            waits=[st["dsKT%d" % tau], ("se", st["vV"]),
                                   ("ve", st.get("vSU%d" % ((oc - 2) % 8), 0))],
                            inc="pe")
                        st["peK"][tau] = pek
                        bk = "pb%d" % (5 + oc % 2)
                        vsu = P.op("vector", lambda e, oc=oc, bank=bank,
                                   tau=tau: e.scalar_tensor_tensor(
                            S_sb[:, oc, :], S_sb[:, oc, :],
                            dec_sb[:, tau, oc:oc + 1], bank[:, :], A.mult,
                            A.add),
                            waits=[("pe", pek)], inc="ve",
                            rd=["S", "dec", bk], wr=["S"])
                        st["vSU%d" % oc] = vsu
                    st["vSU"] = vsu
                    seg = None
                    for h in range(4):
                        seg = P.op("scalar", lambda e, h=h: e.activation(
                            scrA[:, 0:512], pb[h][:, :], AF.Square,
                            accum_out=ms_sb[:, h:h + 1]),
                            waits=[("pe", peo)], inc="se",
                            rd=["pb%d" % h], wr=["scrA", "msA%d" % h])
                    msk = ["msA%d" % i for i in range(4)]
                    P.op("scalar", lambda e: e.activation(
                        ms_sb[:, 4:8], ms_sb[:, 0:4], AF.Ln, scale=1.0 / 512,
                        bias=cst[:, 8:9]),
                        inc="se", rd=msk + ["cst"], wr=["msB"])
                    srf = P.op("scalar", lambda e: e.activation(
                        ms_sb[:, 8:12], ms_sb[:, 4:8], AF.Exp, scale=-0.5),
                        inc="se", rd=["msB"], wr=["msC"])
                    vog = None
                    for h in range(2):
                        vog = P.op("vector", lambda e, h=h, tau=tau:
                                   e.scalar_tensor_tensor(
                            og_tm[:, h * 512:(h + 1) * 512], pb[h][:, :],
                            ms_sb[:, 8 + h:9 + h],
                            sg_sb[:, tau, h * 512:(h + 1) * 512],
                            A.mult, A.mult),
                            waits=[("se", srf), st["dsOG"],
                                   st.get("dsOG1")], inc="ve",
                            rd=["pb%d" % h, "msC", "sg"], wr=["og"])
                    st["dsOG1"] = P.dma("ds", "sync", lambda e:
                                        e.dma_start_transpose(
                        _ap(og_fm, 0, [[16 * 128, 128], [128, 8], [1, 128]]),
                        og_tm[:, 0:1024]),
                        waits=[("ve", vog), ("pe", st["peY"])])
                    for h in range(2, 4):
                        vog = P.op("vector", lambda e, h=h, tau=tau:
                                   e.scalar_tensor_tensor(
                            og_tm[:, h * 512:(h + 1) * 512], pb[h][:, :],
                            ms_sb[:, 8 + h:9 + h],
                            sg_sb[:, tau, h * 512:(h + 1) * 512],
                            A.mult, A.mult),
                            waits=[("se", srf), st["dsOG"]], inc="ve",
                            rd=["pb%d" % h, "msC", "sg"], wr=["og"])
                    st["vOG"] = vog
                    st["dsOG"] = P.dma("ds", "sync", lambda e:
                                       e.dma_start_transpose(
                        _ap(og_fm, 8 * 128,
                            [[16 * 128, 128], [128, 8], [1, 128]]),
                        og_tm[:, 1024:2048]),
                        waits=[("ve", vog), ("pe", st["peY"])])
                    pey = None
                    for dvc in range(16):
                        for half in range(2):
                            bank = pb[4] if half == 0 else pb[7]
                            pey = P.op("tensor", lambda e, dvc=dvc, half=half,
                                       bank=bank: e.matmul(
                                bank[:, :], og_fm[:, dvc, :],
                                wo_sb[:, dvc, half * 512:(half + 1) * 512],
                                start=(dvc == 0), stop=(dvc == 15)),
                                waits=[st["dsOG1"] if dvc < 8 else st["dsOG"],
                                       ("ve", st["vX2"]),
                                       ("ve", st["vS"]),
                                       ("ve", wo_state["last"])],
                                inc="pe" if (dvc == 15 and half == 1)
                                else None)
                    st["peY"] = pey
                    if tau == 1 and sb + 1 < NSB:
                        for s in a_steps(sb + 1, 0):
                            s()
                    vx2 = None
                    for half in range(2):
                        bank = pb[4] if half == 0 else pb[7]
                        vx2 = P.op("vector", lambda e, half=half, bank=bank,
                                   tau=tau: e.scalar_tensor_tensor(
                            xbf[:, tau, half * 512:(half + 1) * 512],
                            bank[:, :],
                            1.0, xbf[:, tau, half * 512:(half + 1) * 512],
                            A.mult, A.add),
                            waits=[("pe", pey)], inc="ve",
                            rd=["pb%d" % (4 if half == 0 else 7), "xbf"],
                            wr=["xbf"])
                    st["vX2"] = vx2
                    st["vX2t%d" % tau] = vx2
                    st["dcX2t%d" % tau] = P.dma(
                        "dc", "scalar", lambda e, c=c, tau=tau: e.dma_start(
                            out=x2_d[c * 128:(c + 1) * 128, :],
                            in_=xbf[:, tau, :]),
                        waits=[("ve", vx2)])
                    if tau == 1 and sb + 1 < NSB:
                        ct2 = build_c(sb + 1)
                        for s in ct2[0][:7]:
                            s()
                        b2 = build_b(sb + 1)
                        b2[0]()
                        b2[3]()
                        nxt = {"ct": ct2, "pre": 7,
                               "b": [b2[1], b2[2], b2[4], b2[5]] + b2[6:]}
                    pump(1)

            while prep_left():
                pump(1)

            pro_sq = {}
            pro_rd = {}
            for tt in range(NT):
                off = (tt % 3) * 1024
                lp = lane_prev[off]
                dl = P.dma("dc", "scalar", lambda e, tt=tt, off=off:
                           e.dma_start(
                    out=qA_t[:, off:off + 1024],
                    in_=x2_d[tt * 128:(tt + 1) * 128, :]),
                    waits=[("ve", pro_rd.get(tt - 3, lp.get("v9", 0))),
                           ("se", pro_sq.get(tt - 3, lp.get("sG", 0))),
                           lp.get("dw0"), lp.get("dt")])
                pro_rd[tt] = P.op(
                    "vector", lambda e, tt=tt, off=off: e.tensor_reduce(
                        st2[:, tt:tt + 1], qA_t[:, off:off + 1024], AX, A.add),
                    waits=[dl], inc="ve", rd=["qA@%d" % off],
                    wr=["p2S%d" % tt])
                pro_sq[tt] = P.op(
                    "scalar", lambda e, tt=tt, off=off: e.activation(
                        scrA[:, :], qA_t[:, off:off + 1024], AF.Square,
                        accum_out=st2[:, 16 + tt:17 + tt]),
                    waits=[dl], inc="se", rd=["qA@%d" % off],
                    wr=["scrA", "p2Q%d" % tt])
            sk = ["p2S%d" % t for t in range(NT)]
            qk = ["p2Q%d" % t for t in range(NT)]
            P.op("vector", lambda e: e.tensor_scalar(
                st2[:, 0:NT], st2[:, 0:NT], 1.0 / H, None, A.mult),
                inc="ve", rd=sk, wr=sk)
            P.op("vector", lambda e: e.tensor_scalar(
                st2[:, 16:16 + NT], st2[:, 16:16 + NT], 1.0 / H, EPS,
                A.mult, A.add), waits=[("se", pro_sq[NT - 1])], inc="ve",
                rd=qk, wr=qk)
            P.op("vector", lambda e: e.scalar_tensor_tensor(
                st2[:, 32:32 + NT], st2[:, 0:NT], 1.0,
                st2[:, 0:NT], A.mult, A.mult), inc="ve", rd=sk, wr=["p2M"])
            vvar2 = P.op("vector", lambda e: e.tensor_tensor(
                st2[:, 16:16 + NT], st2[:, 16:16 + NT],
                st2[:, 32:32 + NT], A.subtract), inc="ve",
                rd=qk + ["p2M"], wr=qk)
            P.op("scalar", lambda e: e.activation(
                st2[:, 32:32 + NT], st2[:, 16:16 + NT], AF.Ln),
                waits=[("ve", vvar2)], inc="se", rd=qk, wr=["p2M"])
            srst = P.op("scalar", lambda e: e.activation(
                st2[:, 48:48 + NT], st2[:, 32:32 + NT], AF.Exp, scale=-0.5),
                inc="se", rd=["p2M"], wr=["p2R"])

        with contextlib.ExitStack() as fstack:
            EF = fstack.enter_context
            wgT_sb = EF(nc.sbuf_tensor("wgTs", [128, NIT, 8, 128], BF16))
            wuT_sb = EF(nc.sbuf_tensor("wuTs", [128, NIT, 8, 128], BF16))
            wdT_sb = EF(nc.sbuf_tensor("wdTs", [128, NIT, 8, 128], BF16))
            x2t = [EF(nc.sbuf_tensor("x2f%d" % i, [128, 1024], BF16))
                   for i in range(2)]
            xq_tm = EF(nc.sbuf_tensor("xqtm", [128, 1024], BF16))
            xq_fm = [EF(nc.sbuf_tensor("xqfm%d" % i, [128, 8, 128], BF16))
                     for i in range(2)]
            hq_tm = EF(nc.sbuf_tensor("hqtm", [128, 2816], BF16))
            hq_fm = EF(nc.sbuf_tensor("hqfm", [128, 22, 128], BF16))
            sgate = [EF(nc.sbuf_tensor("sgate%d" % i, [128, 512], F32))
                     for i in range(2)]
            out_t = EF(nc.sbuf_tensor("outt", [128, 1024], F32))

            barrier = ([("ve", P.v("ve")), ("pe", P.v("pe")), ("se", P.v("se"))]
                       + P.dma_fence("dc") + P.dma_fence("ds") + P.dma_fence("dg"))
            dwg = P.dma("ds", "sync", lambda e: e.dma_start(
                out=wgT_sb[:, :, :, :], in_=wgT_d[:, :, :, :]), waits=barrier)
            dwu = P.dma("ds", "sync", lambda e: e.dma_start(
                out=wuT_sb[:, :, :, :], in_=wuT_d[:, :, :, :]))
            dwd = P.dma("ds", "sync", lambda e: e.dma_start(
                out=wdT_sb[:, :, :, :], in_=wdT_d[:, :, :, :]))

            IGRP = [(0, 512), (512, 512), (1024, 512), (1536, 512), (2048, 512),
                    (2560, 256)]
            fst = {"peGU": {}, "seSi": {}, "vH": {}, "peD": 0, "vOut": 0,
                   "dsXQ": {}, "dsHQa": {}, "dsHQb": {}, "dsHQc": {},
                   "vOutN": {}, "dcY": {}, "xq": {}}

            def x_steps(tt):
                if tt >= NT:
                    return []
                buf = x2t[tt % 2]
                u = {}
                steps = [lambda: u.__setitem__("dl", P.dma(
                    "ds", "sync", lambda e: e.dma_start(
                        out=buf[:, :], in_=x2_d[tt * 128:(tt + 1) * 128, :]),
                    waits=[("ve", fst["vOutN"].get(tt - 2, 0)), ("se", srst)]))]
                steps.append(lambda: u.__setitem__("vln", P.op(
                    "vector", lambda e: e.tensor_scalar(
                        xq_tm[:, :], buf[:, :], st2[:, tt:tt + 1],
                        st2[:, 48 + tt:49 + tt], A.subtract, A.mult),
                    waits=[u["dl"], fst["dsXQ"].get(tt - 1)], inc="ve",
                    rd=["x2f%d" % (tt % 2), "p2S%d" % tt, "p2R"],
                    wr=["xqtm"])))
                qsteps, qm = quantize_steps(
                    0, 1024, (xq_tm, 0, 1024), (xq_tm, 0, 1024),
                    waits=lambda: [("ve", u["vln"])],
                    dst_waits=lambda: [fst["dsXQ"].get(tt - 1)])
                steps += qsteps
                fst["xq"][tt] = qm

                def dmat():
                    fst["dsXQ"][tt] = P.dma(
                        "ds", "sync", lambda e: e.dma_start_transpose(
                            xq_fm[tt % 2][:, :, :], xq_tm[:, :]),
                        waits=[("ve", qm["vq"]),
                               ("pe", fst["peGU"].get(tt - 2, 0)), dwg, dwu])
                steps.append(dmat)
                return steps

            def h_steps(tt):
                buf = x2t[tt % 2]
                steps = []
                for gi, (i0, w) in enumerate(IGRP):
                    pg = pb[2 * (gi % 2)]
                    pu = pb[2 * (gi % 2) + 1]
                    it0 = i0 // 128
                    nt_ = w // 128

                    def mm(gi=gi, pg=pg, pu=pu, it0=it0, nt_=nt_, w=w):
                        pegu = None
                        for hc in range(8):
                            movg = _ap(wgT_sb, it0 * 8 * 128 + hc * 128,
                                       [[NIT * 8 * 128, 128], [8 * 128, nt_],
                                        [1, 128]])
                            movu = _ap(wuT_sb, it0 * 8 * 128 + hc * 128,
                                       [[NIT * 8 * 128, 128], [8 * 128, nt_],
                                        [1, 128]])
                            P.op("tensor", lambda e, hc=hc, movg=movg: e.matmul(
                                pg[:, 0:w], xq_fm[tt % 2][:, hc, :], movg,
                                start=(hc == 0), stop=(hc == 7)),
                                waits=[fst["dsXQ"][tt],
                                       ("se", fst["seSi"].get((tt, gi - 2),
                                        fst["seSi"].get((tt - 1, 5), 0))),
                                       ("ve", fst["vH"].get((tt, gi - 2),
                                        fst["vH"].get((tt - 1, 5), 0)))],
                                inc=None)
                            pegu = P.op("tensor", lambda e, hc=hc, movu=movu:
                                        e.matmul(
                                pu[:, 0:w], xq_fm[tt % 2][:, hc, :], movu,
                                start=(hc == 0), stop=(hc == 7)),
                                inc="pe" if hc == 7 else None)
                        fst["peGU"][tt] = pegu

                    def silu(gi=gi, pg=pg, w=w):
                        fst["seSi"][(tt, gi)] = P.op(
                            "scalar", lambda e: e.activation(
                                sgate[gi % 2][:, 0:w], pg[:, 0:w], AF.Silu),
                            waits=[("pe", fst["peGU"][tt]),
                                   ("ve", fst["vH"].get((tt, gi - 2),
                                    fst["vH"].get((tt - 1, 5), 0)))],
                            inc="se", rd=["pb%d" % (2 * (gi % 2))],
                            wr=["sgate%d" % (gi % 2)])

                    def vh(gi=gi, pu=pu, w=w, i0=i0):
                        fst["vH"][(tt, gi)] = P.op(
                            "vector", lambda e: e.tensor_tensor(
                                hq_tm[:, i0:i0 + w], sgate[gi % 2][:, 0:w],
                                pu[:, 0:w], A.mult),
                            waits=[("se", fst["seSi"][(tt, gi)]),
                                   fst["dsHQ" + "abc"[gi // 2]].get(tt - 1),
                                   ("se", fst.get("seQ2", 0))], inc="ve",
                            rd=["sgate%d" % (gi % 2),
                                "pb%d" % (2 * (gi % 2) + 1)],
                            wr=["hqtm"])

                    steps += [mm, silu, vh]
                qa_s, m_a = quantize_steps(
                    1024, 1024, (hq_tm, 0, 2816), (hq_tm, 0, 2816),
                    waits=lambda: [("ve", fst["vH"][(tt, 1)])],
                    dst_waits=())
                qb_s, m_b = quantize_steps(
                    2048, 1024, (hq_tm, 1024, 2816), (hq_tm, 1024, 2816),
                    waits=lambda: [("ve", fst["vH"][(tt, 3)])],
                    dst_waits=())
                qc_s, m_c = quantize_steps(
                    1024, 768, (hq_tm, 2048, 2816), (hq_tm, 2048, 2816),
                    waits=lambda: [("ve", fst["vH"][(tt, 5)])],
                    dst_waits=())
                u = {}
                grp = steps
                steps = grp[:6]

                def zip_into(dst, big, small, ratio):
                    b = s = 0
                    while b < len(big) or s < len(small):
                        for _ in range(ratio):
                            if b < len(big):
                                dst.append(big[b])
                                b += 1
                        if s < len(small):
                            dst.append(small[s])
                            s += 1

                zip_into(steps, qa_s[:8], grp[6:12], 1)
                ab = []
                ai = bi2 = 0
                qa_rest = qa_s[8:]
                while ai < len(qa_rest) or bi2 < len(qb_s):
                    if ai < len(qa_rest):
                        ab.append(qa_rest[ai])
                        ai += 1
                    if bi2 < len(qb_s):
                        ab.append(qb_s[bi2])
                        bi2 += 1
                zip_into(steps, ab, grp[12:18], 6)

                def dmat_a():
                    u["da"] = P.dma(
                        "ds", "sync", lambda e: e.dma_start_transpose(
                            _ap(hq_fm, 0, [[22 * 128, 128], [128, 8], [1, 128]]),
                            hq_tm[:, 0:1024]),
                        waits=[("ve", m_a["vq"]), ("pe", fst["peD"]), dwd])
                    fst["dsHQa"][tt] = u["da"]
                steps.append(dmat_a)

                def dn_a():
                    ped = None
                    for ic in range(8):
                        for half in range(2):
                            bank = pb[4 + half]
                            mov = _ap(wdT_sb, ic * 8 * 128 + half * 4 * 128,
                                      [[NIT * 8 * 128, 128], [1, 512]])
                            ped = P.op("tensor", lambda e, bank=bank, mov=mov,
                                       ic=ic: e.matmul(
                                bank[:, :], hq_fm[:, ic, :], mov,
                                start=(ic == 0), stop=False),
                                waits=[u["da"], ("ve", fst["vOut"])],
                                inc="pe" if (ic == 7 and half == 1) else None)
                    u["pa"] = ped
                steps.append(dn_a)

                def dmat_b():
                    u["db"] = P.dma(
                        "ds", "sync", lambda e: e.dma_start_transpose(
                            _ap(hq_fm, 8 * 128,
                                [[22 * 128, 128], [128, 8], [1, 128]]),
                            hq_tm[:, 1024:2048]),
                        waits=[("ve", m_b["vq"]), ("pe", fst["peD"])])
                    fst["dsHQb"][tt] = u["db"]
                steps.append(dmat_b)

                def dn_b():
                    ped = None
                    for ic in range(8, 16):
                        for half in range(2):
                            bank = pb[4 + half]
                            mov = _ap(wdT_sb, ic * 8 * 128 + half * 4 * 128,
                                      [[NIT * 8 * 128, 128], [1, 512]])
                            ped = P.op("tensor", lambda e, bank=bank, mov=mov,
                                       ic=ic: e.matmul(
                                bank[:, :], hq_fm[:, ic, :], mov,
                                start=False, stop=False),
                                waits=[u["db"]],
                                inc="pe" if (ic == 15 and half == 1) else None)
                    u["pb"] = ped
                steps.append(dn_b)
                steps += qc_s
                steps.append(lambda: fst.__setitem__("seQ2", m_c["s7"]))

                def dmat_c():
                    u["dc"] = P.dma(
                        "ds", "sync", lambda e: e.dma_start_transpose(
                            _ap(hq_fm, 16 * 128,
                                [[22 * 128, 128], [128, 6], [1, 128]]),
                            hq_tm[:, 2048:2816]),
                        waits=[("ve", m_c["vq"]), ("pe", fst["peD"])])
                    fst["dsHQc"][tt] = u["dc"]
                steps.append(dmat_c)

                def dn_c():
                    ped = None
                    for ic in range(16, NIT):
                        for half in range(2):
                            bank = pb[4 + half]
                            mov = _ap(wdT_sb, ic * 8 * 128 + half * 4 * 128,
                                      [[NIT * 8 * 128, 128], [1, 512]])
                            ped = P.op("tensor", lambda e, bank=bank, mov=mov,
                                       ic=ic: e.matmul(
                                bank[:, :], hq_fm[:, ic, :], mov,
                                start=False, stop=(ic == NIT - 1)),
                                waits=[u["dc"]],
                                inc="pe" if (ic == NIT - 1 and half == 1)
                                else None)
                    fst["peD"] = ped
                steps.append(dn_c)

                def vout():
                    vo = None
                    for half in range(2):
                        bank = pb[4 + half]
                        vo = P.op("vector", lambda e, half=half, bank=bank:
                                  e.scalar_tensor_tensor(
                            out_t[:, half * 512:(half + 1) * 512], bank[:, :],
                            1.0, buf[:, half * 512:(half + 1) * 512],
                            A.mult, A.add),
                            waits=[("pe", fst["peD"]),
                                   fst["dcY"].get(tt - 1)],
                            inc="ve", rd=["pb%d" % (4 + half),
                                          "x2f%d" % (tt % 2)],
                            wr=["outt"])
                    fst["vOut"] = vo
                    fst["vOutN"][tt] = vo
                    fst["dcY"][tt] = P.dma("dc", "scalar", lambda e:
                                           e.dma_start(
                        out=y_d[tt * 128:(tt + 1) * 128, :], in_=out_t[:, :]),
                        waits=[("ve", vo)])
                steps.append(vout)
                return steps

            def zip_emit(big, small, ratio=3):
                bi, si = list(big), list(small)
                b = s = 0
                while b < len(bi) or s < len(si):
                    for _ in range(ratio):
                        if b < len(bi):
                            bi[b]()
                            b += 1
                    if s < len(si):
                        si[s]()
                        s += 1

            for s in x_steps(0):
                s()
            for tt in range(NT):
                nxt = x_steps(tt + 1)
                head, rest = nxt[:7], nxt[7:]
                for s in head:
                    s()
                zip_emit(h_steps(tt), rest, 3)

            P.op("gpsimd", lambda e: e.engine_nop(),
                 waits=P.dma_fence("dc") + P.dma_fence("ds") + P.dma_fence("dg"),
                 drain=False)

        with nc.Block() as block:
            @block.tensor
            def _(e):
                for fn in P.ops["tensor"]:
                    fn(e)

            @block.vector
            def _(e):
                for fn in P.ops["vector"]:
                    fn(e)

            @block.scalar
            def _(e):
                for fn in P.ops["scalar"]:
                    fn(e)

            @block.gpsimd
            def _(e):
                for fn in P.ops["gpsimd"]:
                    fn(e)

            @block.sync
            def _(e):
                for fn in P.ops["sync"]:
                    fn(e)

    return nc


def make_consts():
    mask = np.triu(np.ones((128, 128), dtype=np.float32))
    return np.tile(mask, (1, 4)).astype(ml_dtypes.bfloat16)


def kernel(L0_attn_hidden_states, L0_attn_Wq, L0_attn_Wk, L0_attn_Wv, L0_attn_Wg,
           L0_attn_Wgk1, L0_attn_Wgk2, L0_attn_bgk2, L0_attn_g_norm_w, L0_attn_Wo,
           L0_ffn_w_gate, L0_ffn_w_up, L0_ffn_w_down):
    hs = np.asarray(L0_attn_hidden_states)
    B, T, _ = hs.shape
    nc = build_program(T)
    bgk2l = np.ascontiguousarray(
        np.asarray(L0_attn_bgk2, dtype=np.float32).reshape(8, 128).T)
    gnwl = np.ascontiguousarray(
        np.tile(np.asarray(L0_attn_g_norm_w, dtype=np.float32), 4).reshape(16, 128).T)
    maskT = make_consts()
    common = {
        "Wq": np.ascontiguousarray(L0_attn_Wq),
        "Wk": np.ascontiguousarray(L0_attn_Wk),
        "Wv": np.ascontiguousarray(L0_attn_Wv),
        "Wg": np.ascontiguousarray(L0_attn_Wg),
        "Wgk1": np.ascontiguousarray(L0_attn_Wgk1),
        "Wgk2": np.ascontiguousarray(L0_attn_Wgk2),
        "bgk2l": bgk2l, "gnwl": gnwl,
        "Wo": np.ascontiguousarray(L0_attn_Wo),
        "w_gate": np.ascontiguousarray(L0_ffn_w_gate),
        "w_up": np.ascontiguousarray(L0_ffn_w_up),
        "w_down": np.ascontiguousarray(L0_ffn_w_down),
        "maskT": maskT,
    }
    in_maps = [dict(common, x=np.ascontiguousarray(hs[i])) for i in range(B)]
    res = run_bass_kernel_spmd(nc, in_maps, core_ids=list(range(B)))
    out = np.stack([np.asarray(res.results[i]["y"]) for i in range(B)], axis=0)
    return out.astype(np.float32)


# version 28 (final)
# speedup vs baseline: 1.0162x; 1.0162x over previous
import bisect
import contextlib
import os
import numpy as np
import ml_dtypes
import concourse.bass as bass
import concourse.mybir as mybir
from concourse.bass_utils import run_bass_kernel_spmd

F32 = mybir.dt.float32
BF16 = mybir.dt.bfloat16
F16 = mybir.dt.float16
U8 = mybir.dt.uint8
A = mybir.AluOpType
AF = mybir.ActivationFunctionType
AX = mybir.AxisListType.X

H = 1024
DK = 1024
DV = 2048
LR = 16
IF = 2816
NIT = IF // 128
EPS = 1e-5
LN16 = float(np.log(16.0))
CH = 128
QMID = (0.25, 0.75, 1.25, 1.75, 2.5, 3.5, 5.0)

ENGS = ("tensor", "vector", "scalar", "gpsimd", "sync")
NDMA = 6
FULL_DRAINS = (os.environ.get("BASS_SIM_FP64") == "1"
               or os.environ.get("BASS_FORCE_DRAINS") == "1")
SMALL_KEYS = frozenset(
    ["qmx", "qsc", "qrsc", "dec", "t1", "cst", "ones", "gnw", "bgk",
     "lnS0", "lnS1", "lnQ0", "lnQ1", "lnM", "lnR", "ks",
     "lnM0", "lnM1", "lnR0", "lnR1",
     "msA0", "msA1", "msA2", "msA3", "msB", "msC", "p2M", "p2R"]
    + ["sig%d" % i for i in range(8)] + ["cs%d" % i for i in range(8)]
    + ["p2S%d" % i for i in range(16)] + ["p2Q%d" % i for i in range(16)])


def _ksmall(k):
    return k.split("@")[0] in SMALL_KEYS


class Prog:

    def __init__(self, nc, sems):
        self.nc = nc
        self.sems = sems
        self.cnt = {n: 0 for n in sems}
        self.ops = {e: [] for e in ENGS}
        self.last_wait = {}
        self.dma_idx = {"dg": 0, "ds": 0, "dc": 0}
        self.know = {e: {} for e in ENGS}
        self.semrec = {n: ([], []) for n in sems}
        self.nops = {e: 0 for e in ENGS}
        self.hist = {e: [] for e in ENGS}
        self.ndrains = {e: 0 for e in ENGS}

    @staticmethod
    def _merge(dst, src):
        for k, v in src.items():
            if dst.get(k, 0) < v:
                dst[k] = v

    def op(self, eng, fn, waits=(), inc=None, inc_amt=1, drain=None,
           rd=None, wr=None):
        know = self.know[eng]
        ws = []
        for w in waits:
            if w is None:
                continue
            n, v = w
            if v <= 0:
                continue
            vals, cums = self.semrec[n]
            i = bisect.bisect_right(vals, v) - 1
            if i >= 0:
                self._merge(know, cums[i])
            key = (eng, n)
            if self.last_wait.get(key, -1) >= v:
                continue
            self.last_wait[key] = v
            ws.append((self.sems[n], v))
        idx = self.nops[eng]
        pre_drain = False
        if drain is None and eng in ("vector", "scalar", "gpsimd"):
            floor = 0 if FULL_DRAINS else know.get(eng, 0)
            h = [t for t in self.hist[eng] if t[0] >= floor]
            if FULL_DRAINS or (rd is None and wr is None):
                pre_drain = bool(h)
                rdl, wrl, ann, big = ("*",), ("*",), False, False
            else:
                rdl, wrl, ann = tuple(rd or ()), tuple(wr or ()), True
                rds, wrs = set(rdl), set(wrl)
                big = any(not _ksmall(k) for k in rdl + wrl)
                nbig = sum(1 for t in h if t[4])
                seen_big = 0
                for (j, jr, jw, jann, jbig) in h:
                    if not jann:
                        pre_drain = True
                        break
                    jws = set(jw)
                    conf = (rds & jws) | (wrs & jws) | (wrs & set(jr))
                    if (conf and any(_ksmall(k) for k in conf)
                            and nbig - seen_big - (1 if jbig else 0) < 2):
                        pre_drain = True
                        break
                    if jbig:
                        seen_big += 1
            if pre_drain:
                know[eng] = idx
                h = []
                self.ndrains[eng] += 1
            self.hist[eng] = h + [(idx, rdl, wrl, ann, big)]
        self.nops[eng] = idx + 1
        if inc is not None:
            self.cnt[inc] += inc_amt
            snap = dict(know)
            if snap.get(eng, 0) < idx + 1:
                snap[eng] = idx + 1
            vals, cums = self.semrec[inc]
            base = dict(cums[-1]) if cums else {}
            self._merge(base, snap)
            vals.append(self.cnt[inc])
            cums.append(base)
        sem = self.sems[inc] if inc is not None else None

        def runner(e, fn=fn, ws=ws, sem=sem, amt=inc_amt, pre=pre_drain):
            if pre:
                e.drain()
            for s, v in ws:
                e.wait_ge(s, v)
            ins = fn(e)
            if sem is not None:
                ins.then_inc(sem, amt)

        self.ops[eng].append(runner)
        return self.cnt[inc] if inc is not None else None

    def dma(self, q, eng, fn, waits=()):
        i = self.dma_idx[q]
        self.dma_idx[q] = i + 1
        slot = "%s%d" % (q, i % NDMA)
        ws = list(waits) + [(slot, self.cnt[slot])]
        self.op(eng, fn, waits=ws, inc=slot, inc_amt=16, drain=False)
        return (slot, self.cnt[slot])

    def dma_fence(self, q):
        return [("%s%d" % (q, i), self.cnt["%s%d" % (q, i)]) for i in range(NDMA)]

    def v(self, name):
        return self.cnt[name]


def _ap(t, offset, dims):
    return bass.AP(t, offset, dims)


def build_program(T=2048, dbg=False):
    SB = min(256, T)
    NSB = T // SB
    CPS = SB // CH
    NT = T // 128
    nc = bass.Bass("TRN2", target_bir_lowering=False)

    x_d = nc.dram_tensor("x", [T, H], F32, kind="ExternalInput").ap()
    wq_d = nc.dram_tensor("Wq", [H, DK], F32, kind="ExternalInput").ap()
    wk_d = nc.dram_tensor("Wk", [H, DK], F32, kind="ExternalInput").ap()
    wv_d = nc.dram_tensor("Wv", [H, DV], F32, kind="ExternalInput").ap()
    wg_d = nc.dram_tensor("Wg", [H, DV], F32, kind="ExternalInput").ap()
    wgk1_d = nc.dram_tensor("Wgk1", [H, LR], F32, kind="ExternalInput").ap()
    wgk2_d = nc.dram_tensor("Wgk2", [LR, DK], F32, kind="ExternalInput").ap()
    bgk2_d = nc.dram_tensor("bgk2l", [128, 8], F32, kind="ExternalInput").ap()
    gnw_d = nc.dram_tensor("gnwl", [128, 16], F32, kind="ExternalInput").ap()
    wo_d = nc.dram_tensor("Wo", [DV, H], F32, kind="ExternalInput").ap()
    wgate_d = nc.dram_tensor("w_gate", [IF, H], F32, kind="ExternalInput").ap()
    wup_d = nc.dram_tensor("w_up", [IF, H], F32, kind="ExternalInput").ap()
    wdown_d = nc.dram_tensor("w_down", [H, IF], F32, kind="ExternalInput").ap()
    mask_d = nc.dram_tensor("maskT", [128, 512], BF16, kind="ExternalInput").ap()
    y_d = nc.dram_tensor("y", [T, H], F32, kind="ExternalOutput").ap()
    x2_d = nc.dram_tensor("x2d", [T, H], BF16,
                          kind="ExternalOutput" if dbg else "Internal").ap()
    wqkb_d = nc.dram_tensor("wqkb", [2, 1024, 1024], BF16).ap()
    wgT_d = nc.dram_tensor("wgTd", [128, NIT, 8, 128], BF16).ap()
    wuT_d = nc.dram_tensor("wuTd", [128, NIT, 8, 128], BF16).ap()
    wdT_d = nc.dram_tensor("wdTd", [128, NIT, 8, 128], BF16).ap()

    sem_names = (["pe", "ve", "se", "ge"]
                 + ["%s%d" % (q, i) for q in ("dg", "ds", "dc")
                    for i in range(NDMA)])

    with contextlib.ExitStack() as stack:
        E = stack.enter_context
        sems = {n: E(nc.semaphore(n)) for n in sem_names}
        P = Prog(nc, sems)
        pb = [E(nc.psum_tensor("pb%d" % i, [128, 512], F32)) for i in range(8)]

        cst = E(nc.sbuf_tensor("cst", [128, 12], F32))
        gnw = E(nc.sbuf_tensor("gnw", [128, 16], F32))
        bgk2 = E(nc.sbuf_tensor("bgk2s", [128, 8], F32))
        SCR = 3072
        qa_t = E(nc.sbuf_tensor("qa", [128, SCR], BF16))
        qA_t = E(nc.sbuf_tensor("qA", [128, SCR], BF16))
        qB_t = E(nc.sbuf_tensor("qB", [128, SCR], BF16))
        qP_t = E(nc.sbuf_tensor("qP", [128, SCR], BF16))
        mx_t = E(nc.sbuf_tensor("mx", [128, SCR // 16], F32))
        st2 = E(nc.sbuf_tensor("st2", [128, 4 * 16], F32))
        sc_t = E(nc.sbuf_tensor("sc", [128, SCR // 16], F32))
        rsc_t = E(nc.sbuf_tensor("rsc", [128, SCR // 16], F32))

        def quantize_steps(off, F, src_ap, dst_ap, waits=(), dst_waits=()):
            (st_, so, ss) = src_ap
            (dt_, do, ds_) = dst_ap
            G = F // 16
            g0 = off // 16
            L = "@%d" % off
            ksrc, kdst = st_.name, dt_.name
            m = {}
            src2 = _ap(st_, so, [[ss, 128], [1, F]])
            src3 = _ap(st_, so, [[ss, 128], [16, G], [1, 16]])
            qa2 = qa_t[:, off:off + F]
            qa3 = _ap(qa_t, off, [[SCR, 128], [16, G], [1, 16]])
            qA2 = qA_t[:, off:off + F]
            qB2 = qB_t[:, off:off + F]
            qP2 = qP_t[:, off:off + F]
            qP3 = _ap(qP_t, off, [[SCR, 128], [16, G], [1, 16]])
            rsc_b = _ap(rsc_t, g0, [[SCR // 16, 128], [1, G], [0, 16]])
            sc_b = _ap(sc_t, g0, [[SCR // 16, 128], [1, G], [0, 16]])
            dst3 = _ap(dt_, do, [[ds_, 128], [16, G], [1, 16]])
            W = lambda w: list(w() if callable(w) else w)
            S = []
            S.append(lambda: m.__setitem__("vR", P.op(
                "vector", lambda e: e.tensor_reduce(
                    mx_t[:, g0:g0 + G], src3, AX, A.max,
                    apply_absolute_value=True),
                waits=W(waits), inc="ve", rd=[ksrc], wr=["qmx" + L])))
            S.append(lambda: P.op("vector", lambda e: e.tensor_scalar(
                sc_t[:, g0:g0 + G], mx_t[:, g0:g0 + G],
                (1.0 / 6.0) * (1.0 + 2.0 ** -44), 1e-12 * (1.0 + 2.0 ** -44),
                A.mult, A.max), inc="ve", rd=["qmx" + L], wr=["qsc" + L]))
            S.append(lambda: P.op("vector", lambda e: e.reciprocal(
                rsc_t[:, g0:g0 + G], sc_t[:, g0:g0 + G]),
                inc="ve", rd=["qsc" + L], wr=["qrsc" + L]))
            S.append(lambda: m.__setitem__("sA", P.op(
                "scalar", lambda e: e.activation(qa2, src2, AF.Abs),
                waits=W(waits) + [("ve", m["vR"])], inc="se",
                rd=[ksrc], wr=["qa" + L])))
            S.append(lambda: m.__setitem__("sG", P.op(
                "scalar", lambda e: e.activation(src2, src2, AF.Sign),
                waits=[("ve", m["vR"])] + W(dst_waits), inc="se",
                rd=[ksrc], wr=[ksrc])))
            S.append(lambda: m.__setitem__("v1", P.op(
                "vector", lambda e: e.tensor_tensor(qa3, qa3, rsc_b, A.mult),
                waits=[("se", m["sA"])], inc="ve",
                rd=["qa" + L, "qrsc" + L], wr=["qa" + L])))
            S.append(lambda: m.__setitem__("s1", P.op(
                "scalar", lambda e: e.activation(
                    qA2, qa2, AF.Sign, bias=cst[:, 1:2]),
                waits=[("ve", m["v1"])], inc="se",
                rd=["qa" + L, "cst"], wr=["qA" + L])))
            S.append(lambda: m.__setitem__("s2", P.op(
                "scalar", lambda e: e.activation(
                    qB2, qa2, AF.Sign, bias=cst[:, 2:3]), inc="se",
                rd=["qa" + L, "cst"], wr=["qB" + L])))
            S.append(lambda: m.__setitem__("v2", P.op(
                "vector", lambda e: e.tensor_tensor(qP2, qA2, qB2, A.add),
                waits=[("se", m["s2"])], inc="ve",
                rd=["qA" + L, "qB" + L], wr=["qP" + L])))
            S.append(lambda: m.__setitem__("s3", P.op(
                "scalar", lambda e: e.activation(
                    qA2, qa2, AF.Sign, bias=cst[:, 3:4]),
                waits=[("ve", m["v2"])], inc="se",
                rd=["qa" + L, "cst"], wr=["qA" + L])))
            S.append(lambda: m.__setitem__("s4", P.op(
                "scalar", lambda e: e.activation(
                    qB2, qa2, AF.Sign, bias=cst[:, 4:5]), inc="se",
                rd=["qa" + L, "cst"], wr=["qB" + L])))
            S.append(lambda: m.__setitem__("v3", P.op(
                "vector", lambda e: e.tensor_tensor(qA2, qA2, qB2, A.add),
                waits=[("se", m["s4"])], inc="ve",
                rd=["qA" + L, "qB" + L], wr=["qA" + L])))
            S.append(lambda: m.__setitem__("s5", P.op(
                "scalar", lambda e: e.activation(
                    qB2, qa2, AF.Sign, bias=cst[:, 5:6]),
                waits=[("ve", m["v3"])], inc="se",
                rd=["qa" + L, "cst"], wr=["qB" + L])))
            S.append(lambda: m.__setitem__("v4", P.op(
                "vector", lambda e: e.tensor_tensor(qP2, qP2, qA2, A.add),
                inc="ve", rd=["qP" + L, "qA" + L], wr=["qP" + L])))
            S.append(lambda: m.__setitem__("s6", P.op(
                "scalar", lambda e: e.activation(
                    qA2, qa2, AF.Sign, bias=cst[:, 6:7]),
                waits=[("ve", m["v4"])], inc="se",
                rd=["qa" + L, "cst"], wr=["qA" + L])))
            S.append(lambda: m.__setitem__("v5", P.op(
                "vector", lambda e: e.tensor_tensor(qB2, qB2, qA2, A.add),
                waits=[("se", m["s6"])], inc="ve",
                rd=["qA" + L, "qB" + L], wr=["qB" + L])))
            S.append(lambda: m.__setitem__("s7", P.op(
                "scalar", lambda e: e.activation(
                    qA2, qa2, AF.Sign, bias=cst[:, 7:8]),
                waits=[("ve", m["v5"])], inc="se",
                rd=["qa" + L, "cst"], wr=["qA" + L])))
            S.append(lambda: P.op("vector", lambda e: e.scalar_tensor_tensor(
                qP2, qB2, 2.0, qP2, A.mult, A.add), inc="ve",
                rd=["qB" + L, "qP" + L], wr=["qP" + L]))
            S.append(lambda: P.op("vector", lambda e: e.scalar_tensor_tensor(
                qP2, qA2, 4.0, qP2, A.mult, A.add),
                waits=[("se", m["s7"])], inc="ve",
                rd=["qA" + L, "qP" + L], wr=["qP" + L]))
            S.append(lambda: P.op("vector", lambda e: e.tensor_scalar(
                qP2, qP2, 12.0, 0.25, A.add, A.mult), inc="ve",
                rd=["qP" + L], wr=["qP" + L]))
            S.append(lambda: m.__setitem__("v9", P.op(
                "vector", lambda e: e.tensor_tensor(qP2, qP2, src2, A.mult),
                waits=[("se", m["sG"])], inc="ve",
                rd=["qP" + L, ksrc], wr=["qP" + L])))
            S.append(lambda: m.__setitem__("vq", P.op(
                "vector", lambda e: e.tensor_tensor(dst3, qP3, sc_b, A.mult),
                waits=W(dst_waits), inc="ve",
                rd=["qP" + L, "qsc" + L], wr=[kdst])))
            return S, m

        def quantize(off, F, src_ap, dst_ap, waits=(), dst_waits=()):
            steps, m = quantize_steps(off, F, src_ap, dst_ap, waits, dst_waits)
            for s in steps:
                s()
            return m

        def quantize_prep_steps(off, F, m):
            G = F // 16
            g0 = off // 16
            L = "@%d" % off
            qa2 = qa_t[:, off:off + F]
            qa3 = _ap(qa_t, off, [[SCR, 128], [16, G], [1, 16]])
            qA2 = qA_t[:, off:off + F]
            qA3 = _ap(qA_t, off, [[SCR, 128], [16, G], [1, 16]])
            qB2 = qB_t[:, off:off + F]
            qB3 = _ap(qB_t, off, [[SCR, 128], [16, G], [1, 16]])
            qP2 = qP_t[:, off:off + F]
            qP3 = _ap(qP_t, off, [[SCR, 128], [16, G], [1, 16]])
            rsc_b = _ap(rsc_t, g0, [[SCR // 16, 128], [1, G], [0, 16]])
            sc_b = _ap(sc_t, g0, [[SCR // 16, 128], [1, G], [0, 16]])
            S = []
            S.append(lambda: m.__setitem__("vR", P.op(
                "vector", lambda e: e.tensor_reduce(
                    mx_t[:, g0:g0 + G], qA3, AX, A.max,
                    apply_absolute_value=True),
                waits=[m["dw"]], inc="ve", rd=["qA" + L], wr=["qmx" + L])))
            S.append(lambda: P.op("vector", lambda e: e.tensor_scalar(
                sc_t[:, g0:g0 + G], mx_t[:, g0:g0 + G],
                (1.0 / 6.0) * (1.0 + 2.0 ** -44), 1e-12 * (1.0 + 2.0 ** -44),
                A.mult, A.max), inc="ve", rd=["qmx" + L], wr=["qsc" + L]))
            S.append(lambda: P.op("vector", lambda e: e.reciprocal(
                rsc_t[:, g0:g0 + G], sc_t[:, g0:g0 + G]),
                inc="ve", rd=["qsc" + L], wr=["qrsc" + L]))
            S.append(lambda: m.__setitem__("sA", P.op(
                "scalar", lambda e: e.activation(qa2, qA2, AF.Abs),
                waits=[m["dw"], ("ve", m["vR"])], inc="se",
                rd=["qA" + L], wr=["qa" + L])))
            S.append(lambda: m.__setitem__("sG", P.op(
                "scalar", lambda e: e.activation(qA2, qA2, AF.Sign),
                waits=[("ve", m["vR"])], inc="se",
                rd=["qA" + L], wr=["qA" + L])))
            S.append(lambda: m.__setitem__("v1", P.op(
                "vector", lambda e: e.tensor_tensor(qa3, qa3, rsc_b, A.mult),
                waits=[("se", m["sA"])], inc="ve",
                rd=["qa" + L, "qrsc" + L], wr=["qa" + L])))
            S.append(lambda: m.__setitem__("s1", P.op(
                "scalar", lambda e: e.activation(
                    qP2, qa2, AF.Sign, bias=cst[:, 1:2]),
                waits=[("ve", m["v1"])], inc="se",
                rd=["qa" + L, "cst"], wr=["qP" + L])))
            for j, (wj, opj) in enumerate(((1.0, "add"), (1.0, "add"),
                                           (1.0, "add"), (2.0, "stt"),
                                           (2.0, "stt"), (4.0, "stt")),
                                          start=2):
                def s_j(j=j):
                    m["s%d" % j] = P.op(
                        "scalar", lambda e: e.activation(
                            qB2, qa2, AF.Sign, bias=cst[:, j:j + 1]),
                        waits=[("ve", m.get("v%d" % (j - 1), m["v1"])),
                               m.get("dtprev")], inc="se",
                        rd=["qa" + L, "cst"], wr=["qB" + L])
                S.append(s_j)

                def v_j(j=j, wj=wj, opj=opj):
                    if opj == "add":
                        m["v%d" % j] = P.op(
                            "vector", lambda e: e.tensor_tensor(
                                qP2, qP2, qB2, A.add),
                            waits=[("se", m["s%d" % j])], inc="ve",
                            rd=["qP" + L, "qB" + L], wr=["qP" + L])
                    else:
                        m["v%d" % j] = P.op(
                            "vector", lambda e, wj=wj: e.scalar_tensor_tensor(
                                qP2, qB2, wj, qP2, A.mult, A.add),
                            waits=[("se", m["s%d" % j])], inc="ve",
                            rd=["qP" + L, "qB" + L], wr=["qP" + L])
                S.append(v_j)
            S.append(lambda: P.op("vector", lambda e: e.tensor_scalar(
                qP2, qP2, 12.0, 0.25, A.add, A.mult), inc="ve",
                rd=["qP" + L], wr=["qP" + L]))
            S.append(lambda: m.__setitem__("v9", P.op(
                "vector", lambda e: e.tensor_tensor(qP2, qP2, qA2, A.mult),
                waits=[("se", m["sG"])], inc="ve",
                rd=["qP" + L, "qA" + L], wr=["qP" + L])))
            S.append(lambda: m.__setitem__("vq", P.op(
                "vector", lambda e: e.tensor_tensor(qB3, qP3, sc_b, A.mult),
                waits=[m.get("dtprev")], inc="ve",
                rd=["qP" + L, "qsc" + L], wr=["qB" + L])))
            return S

        with contextlib.ExitStack() as astack:
            EA = astack.enter_context
            ones = EA(nc.sbuf_tensor("ones", [128, 128], BF16))
            maskT = EA(nc.sbuf_tensor("maskTs", [128, 512], BF16))
            S_sb = EA(nc.sbuf_tensor("S", [128, 8, 512], BF16))
            scrA = EA(nc.sbuf_tensor("scrA", [128, 1024], BF16))
            wv_sb = EA(nc.sbuf_tensor("wv", [128, 8, 2048], BF16))
            wg_sb = EA(nc.sbuf_tensor("wg", [128, 8, 2048], BF16))
            wo_sb = EA(nc.sbuf_tensor("wo", [128, 16, 1024], BF16))
            wgk1_sb = EA(nc.sbuf_tensor("wgk1", [128, 8, 16], BF16))
            wgk2_sb = EA(nc.sbuf_tensor("wgk2", [16, 1024], BF16))
            wqs = [EA(nc.sbuf_tensor("wqs%d" % i, [128, 8, 128], BF16))
                   for i in range(2)]
            xbf = EA(nc.sbuf_tensor("xbf", [128, CPS, 1024], BF16))
            ln1f = EA(nc.sbuf_tensor("ln1f", [128, CPS, 8, 128], BF16))
            ln1t = EA(nc.sbuf_tensor("ln1t", [128, 1024], BF16))
            q_sb = EA(nc.sbuf_tensor("qsb", [128, 8, SB], BF16))
            k_sb = EA(nc.sbuf_tensor("ksb", [128, 8, SB], BF16))
            kT_sb = EA(nc.sbuf_tensor("kT", [128, 8 * CPS, 128], BF16))
            v_sb = EA(nc.sbuf_tensor("vsb", [128, 2048], BF16))
            sg_sb = EA(nc.sbuf_tensor("sgsb", [128, CPS, 2048], BF16))
            sig_sb = EA(nc.sbuf_tensor("sig", [128, 8, 128], F16))
            cs_sb = EA(nc.sbuf_tensor("cs", [128, 8, 128], F32))
            e_scr = EA(nc.sbuf_tensor("escr", [128, 8, 128], F32))
            dec_sb = EA(nc.sbuf_tensor("dec", [128, CPS, 8], F32))
            lnst = EA(nc.sbuf_tensor("lnst", [128, 16], F32))
            ms_sb = EA(nc.sbuf_tensor("ms", [128, 12], F32))
            ks_sb = EA(nc.sbuf_tensor("ks", [128, 8, 128], BF16))
            t1_sb = EA(nc.sbuf_tensor("t1", [16, 128], BF16))
            sT_sb = EA(nc.sbuf_tensor("sT", [128, 512], BF16))
            og_tm = EA(nc.sbuf_tensor("ogtm", [128, 2048], BF16))
            og_fm = EA(nc.sbuf_tensor("ogfm", [128, 16, 128], BF16))
            wqfm = EA(nc.sbuf_tensor("wqfm", [128, 24, 128], BF16))

            P.op("vector", lambda e: e.memset(ones[:], 1.0), inc="ve",
                 rd=[], wr=["ones"])
            P.op("vector", lambda e: e.memset(cst[:, 0:1], -LN16), inc="ve",
                 rd=[], wr=["cst"])
            for j, m in enumerate(QMID):
                P.op("vector", lambda e, j=j, m=m: e.memset(
                    cst[:, 1 + j:2 + j], -m), inc="ve", rd=[], wr=["cst"])
            P.op("vector", lambda e: e.memset(cst[:, 8:9], EPS), inc="ve",
                 rd=[], wr=["cst"])
            P.op("vector", lambda e: e.memset(cst[:, 9:10], 1.2e-38), inc="ve",
                 rd=[], wr=["cst"])
            P.op("vector", lambda e: e.memset(S_sb[:, :, :], 0.0), inc="ve",
                 rd=[], wr=["S"])
            dmask = P.dma("ds", "sync", lambda e: e.dma_start(
                out=maskT[:], in_=mask_d[:, :]))
            dgnw = P.dma("ds", "sync", lambda e: e.dma_start(
                out=gnw[:], in_=gnw_d[:, :]))
            dbg_ = P.dma("ds", "sync", lambda e: e.dma_start(
                out=bgk2[:], in_=bgk2_d[:, :]))
            st = {
                "peV": 0, "vV": 0, "peS": 0, "vS": 0, "peO": 0, "peK": {},
                "vSU": 0, "vOG": 0, "dsOG": None, "peY": 0, "vX2": 0,
                "dcX2": None, "peB": {}, "vB": {}, "dsKT": None, "dsLN": None,
                "vKi": 0, "vt1": 0,
            }
            st["dxpre"] = {}
            for tau in range(CPS):
                st["dxpre"][tau] = P.dma(
                    "dg", "gpsimd", lambda e, tau=tau: e.dma_start(
                        out=xbf[:, tau, :], in_=x_d[tau * 128:(tau + 1) * 128, :]))
            dgK1 = P.dma("dg", "gpsimd", lambda e: e.dma_start(
                out=wgk1_sb[:, :, :],
                in_=_ap(wgk1_d.tensor, 0, [[16, 128], [128 * 16, 8], [1, 16]])))
            dW = P.dma("dg", "gpsimd", lambda e: e.dma_start(
                out=wgk2_sb[:, :], in_=_ap(wgk2_d.tensor, 0, [[1024, 16], [1, 1024]])))
            stg = {}
            for i in range(16):
                wi, rt = divmod(i, 8)
                off = (i % 3) * 1024
                src_d = (wq_d if wi == 0 else wk_d)
                din = P.dma("dg", "gpsimd", lambda e, src_d=src_d, rt=rt,
                            off=off: e.dma_start(
                    out=qA_t[:, off:off + 1024],
                    in_=src_d[rt * 128:(rt + 1) * 128, :]),
                    waits=[stg.get(i - 3)])
                stg[i] = P.dma("dc", "scalar", lambda e, wi=wi, rt=rt, off=off:
                               e.dma_start(
                    out=_ap(wqkb_d.tensor, wi * 1024 * 1024 + rt * 128 * 1024,
                            [[1024, 128], [1, 1024]]),
                    in_=qA_t[:, off:off + 1024]),
                    waits=[din])
            dgWV = P.dma("dg", "gpsimd", lambda e: e.dma_start(
                out=wv_sb[:, :, :],
                in_=_ap(wv_d.tensor, 0, [[2048, 128], [128 * 2048, 8], [1, 2048]])))
            dgWG = P.dma("dg", "gpsimd", lambda e: e.dma_start(
                out=wg_sb[:, :, :],
                in_=_ap(wg_d.tensor, 0, [[2048, 128], [128 * 2048, 8], [1, 2048]])))
            stg_done = [stg[13], stg[14], stg[15]]

            lane_prev = {
                0: {"v9": P.v("ve"), "sG": P.v("se"), "dw0": stg.get(15)},
                1024: {"v9": P.v("ve"), "sG": P.v("se"), "dw0": stg.get(13)},
                2048: {"v9": P.v("ve"), "sG": P.v("se"), "dw0": stg.get(14)}}

            def mk_prep(load_fn, F, out_fn):
                def factory(off):
                    prev = lane_prev[off]
                    nblk = F // 128
                    blk0 = 8 * (off // 1024)
                    qm = {"dtprev": prev.get("dt")}
                    steps = [lambda: qm.__setitem__("dw", P.dma(
                        "dg", "gpsimd", lambda e: e.dma_start(
                            out=qA_t[:, off:off + F], in_=load_fn()),
                        waits=[("ve", prev.get("v9", 0)),
                               ("se", prev.get("sG", 0)),
                               prev.get("dw0")]))]
                    steps += quantize_prep_steps(off, F, qm)
                    steps.append(lambda: qm.__setitem__("dt", P.dma(
                        "ds", "sync", lambda e: e.dma_start_transpose(
                            _ap(wqfm, blk0 * 128,
                                [[24 * 128, 128], [128, nblk], [1, 128]]),
                            qB_t[:, off:off + F]),
                        waits=[("ve", qm["vq"]), prev.get("st")])))
                    steps.append(lambda: qm.__setitem__("st", P.dma(
                        "dc", "scalar", lambda e: e.dma_start(
                            out=out_fn(),
                            in_=_ap(wqfm, blk0 * 128,
                                    [[24 * 128, 128], [128, nblk], [1, 128]])),
                        waits=[qm["dt"]])))
                    lane_prev[off] = qm
                    return steps
                prep_units.append(factory)

            prep_units = []
            wo_state = {"last": 0}

            def mk_wo(dv2):
                def factory(off):
                    prev = lane_prev[off]
                    u = {}

                    def wo_dma():
                        u["d"] = P.dma("dg", "gpsimd", lambda e: e.dma_start(
                            out=qA_t[:, off:off + 1024],
                            in_=_ap(wo_d.tensor, dv2 * 128 * 1024,
                                    [[1024, 128], [1, 1024]])),
                            waits=[("ve", prev.get("v9", 0)),
                                   ("se", prev.get("sG", 0)),
                                   prev.get("dw0"), prev.get("dt")])

                    def wo_ts():
                        m = P.op("vector", lambda e: e.tensor_scalar(
                            wo_sb[:, dv2, :], qA_t[:, off:off + 1024],
                            gnw[:, dv2:dv2 + 1], None, A.mult),
                            waits=[u["d"], dgnw], inc="ve",
                            rd=["qA@%d" % off, "gnw"], wr=["wo"])
                        wo_state["last"] = m
                        lane_prev[off] = {"v9": m, "sG": prev.get("sG", 0),
                                          "dw0": prev.get("dw0"),
                                          "dt": prev.get("dt")}
                    return [wo_dma, wo_ts]
                prep_units.append(factory)

            for dv2 in range(16):
                mk_wo(dv2)

            for w_src, w_dst in ((wgate_d, wgT_d), (wup_d, wuT_d)):
                for it in range(NIT):
                    mk_prep(
                        (lambda w_src=w_src, it=it: _ap(
                            w_src.tensor, it * 128 * 1024,
                            [[1024, 128], [1, 1024]])),
                        1024,
                        (lambda w_dst=w_dst, it=it: w_dst[:, it, :, :]))
            for ht in range(8):
                for sl0, F in ((0, 1024), (1024, 1024), (2048, 768)):
                    it0 = sl0 // 128
                    nblk = F // 128
                    mk_prep(
                        (lambda ht=ht, sl0=sl0, F=F: _ap(
                            wdown_d.tensor, ht * 128 * IF + sl0,
                            [[IF, 128], [1, F]])),
                        F,
                        (lambda ht=ht, it0=it0, nblk=nblk: _ap(
                            wdT_d.tensor, it0 * (8 * 128) + ht * 128,
                            [[NIT * 8 * 128, 128], [8 * 128, nblk], [1, 128]])))

            psched = {"active": {0: [], 1024: [], 2048: []}, "turn": [0],
                      "points": NSB * 17}
            NPREPSTEPS = (len(prep_units) - 16) * 25 + 16 * 2

            def prep_remaining():
                return (len(prep_units) * 24 + len(psched["active"][0])
                        + len(psched["active"][1024])
                        + len(psched["active"][2048]))

            def pump(k=1):
                pts = max(psched["points"], 1)
                psched["points"] -= k
                n = k * max(1, (prep_remaining() + pts - 1) // pts)
                for _ in range(n):
                    done = True
                    for _try in range(3):
                        off = 1024 * psched["turn"][0]
                        psched["turn"][0] = (psched["turn"][0] + 1) % 3
                        act = psched["active"][off]
                        if not act and prep_units:
                            act[:] = prep_units.pop(0)(off)
                        if act:
                            act.pop(0)()
                            done = False
                            break
                    if done:
                        return

            def prep_left():
                return (prep_units or psched["active"][0]
                        or psched["active"][1024] or psched["active"][2048])

            def a_steps(sb, tau):
                c = sb * CPS + tau
                u = {}
                steps = []

                def s_dx():
                    if sb == 0 and st["dxpre"].get(tau) is not None:
                        u["dx"] = st["dxpre"].pop(tau)
                    else:
                        u["dx"] = P.dma(
                            "dg", "gpsimd", lambda e: e.dma_start(
                                out=xbf[:, tau, :],
                                in_=x_d[c * 128:(c + 1) * 128, :]),
                            waits=[("ve", st.get("vX2t%d" % tau, 0)),
                                   st.get("dcX2t%d" % tau)])
                steps.append(s_dx)

                def s_red():
                    u["red"] = P.op("vector", lambda e: e.tensor_reduce(
                        lnst[:, tau:tau + 1], xbf[:, tau, :], AX, A.add),
                        waits=[u["dx"]], inc="ve",
                        rd=["xbf"], wr=["lnS%d" % tau])
                    u["ssq"] = P.op("scalar", lambda e: e.activation(
                        scrA[:, :], xbf[:, tau, :], AF.Square,
                        accum_out=lnst[:, 4 + tau:5 + tau]),
                        waits=[u["dx"]], inc="se",
                        rd=["xbf"], wr=["scrA", "lnQ%d" % tau])
                steps.append(s_red)

                def s_stats():
                    P.op("vector", lambda e: e.tensor_scalar(
                        lnst[:, tau:tau + 1], lnst[:, tau:tau + 1], 1.0 / H,
                        None, A.mult), inc="ve",
                        rd=["lnS%d" % tau], wr=["lnS%d" % tau])
                    P.op("vector", lambda e: e.tensor_scalar(
                        lnst[:, 4 + tau:5 + tau], lnst[:, 4 + tau:5 + tau],
                        1.0 / H, EPS, A.mult, A.add),
                        waits=[("se", u["ssq"])], inc="ve",
                        rd=["lnQ%d" % tau], wr=["lnQ%d" % tau])
                    P.op("vector", lambda e: e.scalar_tensor_tensor(
                        lnst[:, 8 + tau:9 + tau], lnst[:, tau:tau + 1], 1.0,
                        lnst[:, tau:tau + 1], A.mult, A.mult), inc="ve",
                        rd=["lnS%d" % tau], wr=["lnM%d" % tau])
                    u["var"] = P.op("vector", lambda e: e.tensor_tensor(
                        lnst[:, 4 + tau:5 + tau], lnst[:, 4 + tau:5 + tau],
                        lnst[:, 8 + tau:9 + tau], A.subtract), inc="ve",
                        rd=["lnQ%d" % tau, "lnM%d" % tau],
                        wr=["lnQ%d" % tau])
                steps.append(s_stats)

                def s_lnexp():
                    P.op("scalar", lambda e: e.activation(
                        lnst[:, 8 + tau:9 + tau], lnst[:, 4 + tau:5 + tau],
                        AF.Ln), waits=[("ve", u["var"])], inc="se",
                        rd=["lnQ%d" % tau], wr=["lnM%d" % tau])
                    u["sr"] = P.op("scalar", lambda e: e.activation(
                        lnst[:, 12 + tau:13 + tau], lnst[:, 8 + tau:9 + tau],
                        AF.Exp, scale=-0.5), inc="se",
                        rd=["lnM%d" % tau], wr=["lnR%d" % tau])
                steps.append(s_lnexp)

                def s_vl():
                    vl = P.op("vector", lambda e: e.tensor_scalar(
                        ln1t[:, :], xbf[:, tau, :], lnst[:, tau:tau + 1],
                        lnst[:, 12 + tau:13 + tau], A.subtract, A.mult),
                        waits=[("se", u["sr"]), st["dsLN"]], inc="ve",
                        rd=["xbf", "lnS%d" % tau, "lnR%d" % tau], wr=["ln1t"])
                    st["dsLN"] = P.dma("ds", "sync", lambda e:
                                       e.dma_start_transpose(
                        ln1f[:, tau, :, :], ln1t[:, :]),
                        waits=[("ve", vl), ("pe", st["peV"])])
                steps.append(s_vl)
                return steps

            def build_b(sb):
                b_steps = []
                for wi in range(2):
                    for oc in range(8):
                        slot = (wi * 8 + oc) % 2
                        bank = pb[slot]
                        key = (wi, oc)
                        dst = q_sb if wi == 0 else k_sb
                        u = {}

                        def bdma(wi=wi, oc=oc, slot=slot, u=u):
                            q_, e_ = (("dc", "scalar") if slot == 0
                                      else ("ds", "sync"))
                            u["d"] = P.dma(q_, e_, lambda e: e.dma_start(
                                out=wqs[slot][:, :, :],
                                in_=_ap(wqkb_d.tensor,
                                        wi * 1024 * 1024 + oc * 128,
                                        [[1024, 128], [128 * 1024, 8],
                                         [1, 128]])),
                                waits=[("pe", st.get("wqsPE%d" % slot, 0))]
                                + stg_done)

                        def bmm(wi=wi, oc=oc, slot=slot, bank=bank, key=key,
                                u=u):
                            prev_vb = st.get("wqsVB%d" % slot, 0)
                            for hc in range(8):
                                mov = _ap(ln1f, hc * 128,
                                          [[CPS * 8 * 128, 128],
                                           [8 * 128, CPS], [1, 128]])
                                st["peB"][key] = P.op(
                                    "tensor", lambda e, bank=bank, slot=slot,
                                    hc=hc, mov=mov: e.matmul(
                                        bank[:, 0:SB], wqs[slot][:, hc, :],
                                        mov,
                                        start=(hc == 0), stop=(hc == 7)),
                                    waits=[u["d"], st["dsLN"],
                                           ("se", prev_vb),
                                           ("ve", st["vOG"])],
                                    inc="pe" if hc == 7 else None)
                            st["wqsPE%d" % slot] = st["peB"][key]

                        def bev(wi=wi, oc=oc, slot=slot, bank=bank, key=key,
                                dst=dst):
                            st["vB"][key] = P.op(
                                "scalar", lambda e: e.activation(
                                    dst[:, oc, :], bank[:, 0:SB], AF.Copy),
                                waits=[("pe", st["peB"][key])], inc="se",
                                rd=["pb%d" % slot],
                                wr=["qsb" if wi == 0 else "ksb"])
                            st["wqsVB%d" % slot] = st["vB"][key]

                        b_steps += [bdma, bmm, bev]
                return b_steps

            sigk = ["sig%d" % i for i in range(8)]
            csk = ["cs%d" % i for i in range(8)]

            def build_c(sb):
                ctau = {0: [], 1: []}
                for tau in range(CPS):
                    cu = {}

                    def c_t1(tau=tau, cu=cu):
                        peT1 = None
                        for hc in range(8):
                            peT1 = P.op("tensor", lambda e, hc=hc, tau=tau:
                                        e.matmul(
                                pb[2][0:16, 0:128], wgk1_sb[:, hc, :],
                                ln1f[:, tau, hc, :],
                                start=(hc == 0), stop=(hc == 7)),
                                waits=[dW, dgK1, st["dsLN"],
                                       ("se", st["vt1"]),
                                       ("ve", st["vOG"])],
                                inc="pe" if hc == 7 else None)
                        cu["vt1"] = st["vt1"] = P.op(
                            "scalar", lambda e: e.activation(
                                t1_sb[0:16, :], pb[2][0:16, 0:128], AF.Copy),
                            waits=[("pe", peT1)], inc="se",
                            rd=["pb2"], wr=["t1"])
                    ctau[tau].append(c_t1)

                    for grp in range(2):
                        def c_gk(tau=tau, grp=grp, cu=cu):
                            pegk = None
                            for j in range(4):
                                oc = grp * 4 + j
                                pegk = P.op("tensor", lambda e, oc=oc, j=j:
                                            e.matmul(
                                    pb[3][:, j * 128:(j + 1) * 128],
                                    wgk2_sb[0:16, oc * 128:(oc + 1) * 128],
                                    t1_sb[0:16, :], start=True, stop=True),
                                    waits=[("se", cu["vt1"]),
                                           ("se", st.get("seC1v", 0))],
                                    inc="pe" if j == 3 else None)
                            for j in range(4):
                                oc = grp * 4 + j
                                st["seC1v"] = P.op(
                                    "scalar", lambda e, oc=oc, j=j:
                                    e.activation(
                                        sig_sb[:, oc, :],
                                        pb[3][:, j * 128:(j + 1) * 128],
                                        AF.Sigmoid, bias=bgk2[:, oc:oc + 1],
                                        scale=1.0),
                                    waits=[("pe", pegk), dbg_,
                                           ("se", st.get("vclp", 0))],
                                    inc="se", rd=["pb3", "bgk"],
                                    wr=["sig%d" % oc])
                        ctau[tau].append(c_gk)

                    def c_cl(tau=tau, cu=cu):
                        cu["sln"] = st["vclp"] = P.op(
                            "scalar", lambda e: e.activation(
                                e_scr[:, :, :], sig_sb[:, :, :], AF.Ln,
                                bias=cst[:, 9:10]),
                            waits=[("se", st["seC1v"]),
                                   ("ve", st.get("vKi", 0))],
                            inc="se", rd=sigk + ["cst"], wr=["escr"])
                    ctau[tau].append(c_cl)

                    for half in range(2):
                        def c_scan(tau=tau, half=half, cu=cu):
                            for oc in range(half * 4, half * 4 + 4):
                                cu["vscan"] = P.op(
                                    "vector", lambda e, oc=oc:
                                    e.tensor_tensor_scan(
                                        cs_sb[:, oc, :], ones[:, :],
                                        e_scr[:, oc, :], 0.0, A.mult, A.add),
                                    waits=[("se", cu["sln"]),
                                           ("se", st.get("seEmb", 0))],
                                    inc="ve", rd=["ones", "escr"],
                                    wr=["cs%d" % oc])
                        ctau[tau].append(c_scan)

                    def c_dec(tau=tau, cu=cu):
                        st["seDec"] = P.op(
                            "scalar", lambda e, tau=tau: e.activation(
                                dec_sb[:, tau, :],
                                _ap(cs_sb, 127, [[8 * 128, 128], [128, 8]]),
                                AF.Exp, scale=1.0 / 16),
                            waits=[("ve", cu["vscan"]), ("ve", st["vSU"])],
                            inc="se", rd=csk, wr=["dec"])
                        cu["sebq"] = P.op("scalar", lambda e: e.activation(
                            e_scr[:, :, :], cs_sb[:, :, :], AF.Exp,
                            scale=1.0 / 16, bias=cst[:, 0:1]),
                            waits=[("ve", cu["vscan"])], inc="se",
                            rd=csk + ["cst"], wr=["escr"])
                    ctau[tau].append(c_dec)

                    def c_qg(tau=tau, cu=cu):
                        vqg = P.op("vector", lambda e, tau=tau: e.tensor_tensor(
                            _ap(q_sb, tau * 128,
                                [[8 * SB, 128], [SB, 8], [1, 128]]),
                            _ap(q_sb, tau * 128,
                                [[8 * SB, 128], [SB, 8], [1, 128]]),
                            e_scr[:, :, :], A.mult),
                            waits=[("se", cu["sebq"]), ("pe", st["peO"]),
                                   ("se", st["vB"][(0, 7)])], inc="ve",
                            rd=["escr", "qsb"], wr=["qsb"])
                        st["seEmb"] = P.op("scalar", lambda e: e.activation(
                            e_scr[:, :, :], cs_sb[:, :, :], AF.Exp,
                            scale=-1.0 / 16),
                            waits=[("ve", vqg)], inc="se",
                            rd=csk, wr=["escr"])
                        st["vKi"] = P.op(
                            "vector", lambda e, tau=tau: e.tensor_tensor(
                                _ap(k_sb, tau * 128,
                                    [[8 * SB, 128], [SB, 8], [1, 128]]),
                                _ap(k_sb, tau * 128,
                                    [[8 * SB, 128], [SB, 8], [1, 128]]),
                                e_scr[:, :, :], A.mult),
                            waits=[("se", st["seEmb"]), ("pe", st["peO"]),
                                   ("se", st["vB"][(1, 7)])], inc="ve",
                            rd=["escr", "ksb"], wr=["ksb"])
                    ctau[tau].append(c_qg)

                    def c_ks(tau=tau, cu=cu):
                        vks = None
                        for oc in range(8):
                            vks = P.op("scalar", lambda e, oc=oc, tau=tau:
                                       e.activation(
                                ks_sb[:, oc, :],
                                k_sb[:, oc, tau * 128:(tau + 1) * 128],
                                AF.Copy, scale=dec_sb[:, tau, oc:oc + 1]),
                                waits=[("ve", st["vKi"]),
                                       st.get("dsKTlast")], inc="se",
                                rd=["ksb", "dec"], wr=["ks"])
                        st["dsKT%d" % tau] = st["dsKTlast"] = P.dma(
                            "ds", "sync", lambda e, tau=tau:
                            e.dma_start_transpose(
                                _ap(kT_sb, tau * 8 * 128,
                                    [[16 * 128, 128], [128, 8], [1, 128]]),
                                _ap(ks_sb, 0, [[8 * 128, 128], [1, 8 * 128]])),
                            waits=[("se", vks), ("pe", st["peK"].get(tau, 0))])
                    ctau[tau].append(c_ks)
                return ctau

            nxt = {}
            for sb in range(NSB):
                c0 = sb * CPS
                if sb == 0:
                    for s in a_steps(0, 0):
                        s()
                    ct = build_c(0)
                    pre = 0
                else:
                    ct = nxt["ct"]
                    pre = nxt["pre"]
                for s in a_steps(sb, 1):
                    s()
                pump(1)

                b_steps = nxt["b"] if sb else build_b(sb)
                eu = {}

                def e_steps(tau):
                    def e_mm(tau=tau):
                        prev_se = st.get("seE", 0)
                        peg = None
                        for hc in range(8):
                            for b4 in range(4):
                                peg = P.op("tensor", lambda e, hc=hc, b4=b4,
                                           tau=tau: e.matmul(
                                    pb[4 + b4][:, :], ln1f[:, tau, hc, :],
                                    wg_sb[:, hc, b4 * 512:(b4 + 1) * 512],
                                    start=(hc == 0), stop=(hc == 7)),
                                    waits=[dgWG, ("se", prev_se),
                                           ("ve", st["vX2"])],
                                    inc="pe" if (hc == 7 and b4 == 3)
                                    else None)
                        eu[tau] = peg

                    def e_si(tau=tau):
                        for b4 in range(4):
                            st["seE"] = P.op(
                                "scalar", lambda e, b4=b4, tau=tau:
                                e.activation(
                                    sg_sb[:, tau, b4 * 512:(b4 + 1) * 512],
                                    pb[4 + b4][:, :], AF.Silu),
                                waits=[("pe", eu[tau])], inc="se",
                                rd=["pb%d" % (4 + b4)], wr=["sg"])
                    return [e_mm, e_si]

                cpre = ct[0][pre:7] + ct[1][:3]
                ev = e_steps(0) + e_steps(1)
                mixed = []
                while cpre or ev:
                    if cpre:
                        mixed.append(cpre.pop(0))
                    if cpre:
                        mixed.append(cpre.pop(0))
                    if ev:
                        mixed.append(ev.pop(0))
                c_steps = mixed + ct[0][7:] + ct[1][3:]
                gate = len(mixed)
                bi = ci = 0
                while bi < len(b_steps) or ci < len(c_steps):
                    for _ in range(5):
                        if bi < len(b_steps):
                            b_steps[bi]()
                            bi += 1
                    if bi % 15 == 0 or bi >= len(b_steps):
                        pump(1)
                    if ci < len(c_steps) and not (ci >= gate
                                                  and bi < len(b_steps)):
                        c_steps[ci]()
                        ci += 1
                pump(1)

                for tau in range(CPS):
                    c = c0 + tau
                    pev = None
                    for hc in range(8):
                        for b4 in range(4):
                            pev = P.op("tensor", lambda e, hc=hc, b4=b4,
                                       tau=tau: e.matmul(
                                pb[b4][:, :], ln1f[:, tau, hc, :],
                                wv_sb[:, hc, b4 * 512:(b4 + 1) * 512],
                                start=(hc == 0), stop=(hc == 7)),
                                waits=[dgWV, ("ve", st["vOG"]),
                                       ("ve", st["vKi"]),
                                       ("se", st["seE"])],
                                inc="pe" if (hc == 7 and b4 == 3) else None)
                    st["peV"] = pev
                    vv = None
                    for b4 in range(4):
                        vv = P.op("scalar", lambda e, b4=b4: e.activation(
                            v_sb[:, b4 * 512:(b4 + 1) * 512], pb[b4][:, :],
                            AF.Copy),
                            waits=[("pe", pev)], inc="se",
                            rd=["pb%d" % b4], wr=["vsb"])
                    st["vV"] = vv
                    pes = None
                    for h in range(4):
                        for dc in range(2):
                            oc = 2 * h + dc
                            pes = P.op("tensor", lambda e, h=h, oc=oc, dc=dc,
                                       tau=tau: e.matmul(
                                pb[4][:, h * 128:(h + 1) * 128],
                                _ap(k_sb, oc * SB + tau * 128,
                                    [[8 * SB, 128], [1, 128]]),
                                _ap(q_sb, oc * SB + tau * 128,
                                    [[8 * SB, 128], [1, 128]]),
                                start=(dc == 0), stop=(dc == 1)),
                                waits=[("ve", st["vKi"]), ("ve", st["vX2"])],
                                inc="pe" if (h == 3 and dc == 1) else None)
                    st["peS"] = pes
                    vs = P.op("vector", lambda e: e.tensor_tensor(
                        sT_sb[:, :], pb[4][:, :], maskT[:, :], A.mult),
                        waits=[("pe", pes), ("pe", st["peO"]), dmask], inc="ve",
                        rd=["mask", "pb4"], wr=["sT"])
                    st["vS"] = vs
                    peo = None
                    for h in range(4):
                        peo = P.op("tensor", lambda e, h=h: e.matmul(
                            pb[h][:, :], sT_sb[:, h * 128:(h + 1) * 128],
                            v_sb[:, h * 512:(h + 1) * 512], start=True,
                            stop=False),
                            waits=[("ve", vs), ("se", vv), ("ve", st["vSU"])],
                            inc=None)
                        for dc in range(2):
                            oc = 2 * h + dc
                            peo = P.op("tensor", lambda e, h=h, oc=oc, dc=dc,
                                       tau=tau: e.matmul(
                                pb[h][:, :],
                                _ap(q_sb, oc * SB + tau * 128,
                                    [[8 * SB, 128], [1, 128]]),
                                S_sb[:, oc, :], start=False, stop=(dc == 1)),
                                inc="pe" if (h == 3 and dc == 1) else None)
                    st["peO"] = peo
                    vsu = None
                    for oc in range(8):
                        bank = pb[5 + oc % 2]
                        pek = P.op("tensor", lambda e, oc=oc, bank=bank,
                                   tau=tau: e.matmul(
                            bank[:, :], kT_sb[:, tau * 8 + oc, :],
                            v_sb[:, (oc // 2) * 512:(oc // 2 + 1) * 512],
                            start=True, stop=True),
                            waits=[st["dsKT%d" % tau], ("se", st["vV"]),
                                   ("ve", st.get("vSU%d" % ((oc - 2) % 8), 0))],
                            inc="pe")
                        st["peK"][tau] = pek
                        bk = "pb%d" % (5 + oc % 2)
                        vsu = P.op("vector", lambda e, oc=oc, bank=bank,
                                   tau=tau: e.scalar_tensor_tensor(
                            S_sb[:, oc, :], S_sb[:, oc, :],
                            dec_sb[:, tau, oc:oc + 1], bank[:, :], A.mult,
                            A.add),
                            waits=[("pe", pek)], inc="ve",
                            rd=["S", "dec", bk], wr=["S"])
                        st["vSU%d" % oc] = vsu
                    st["vSU"] = vsu
                    seg = None
                    for h in range(4):
                        seg = P.op("scalar", lambda e, h=h: e.activation(
                            scrA[:, 0:512], pb[h][:, :], AF.Square,
                            accum_out=ms_sb[:, h:h + 1]),
                            waits=[("pe", peo)], inc="se",
                            rd=["pb%d" % h], wr=["scrA", "msA%d" % h])
                    msk = ["msA%d" % i for i in range(4)]
                    P.op("scalar", lambda e: e.activation(
                        ms_sb[:, 4:8], ms_sb[:, 0:4], AF.Ln, scale=1.0 / 512,
                        bias=cst[:, 8:9]),
                        inc="se", rd=msk + ["cst"], wr=["msB"])
                    srf = P.op("scalar", lambda e: e.activation(
                        ms_sb[:, 8:12], ms_sb[:, 4:8], AF.Exp, scale=-0.5),
                        inc="se", rd=["msB"], wr=["msC"])
                    vog = None
                    for h in range(2):
                        vog = P.op("vector", lambda e, h=h, tau=tau:
                                   e.scalar_tensor_tensor(
                            og_tm[:, h * 512:(h + 1) * 512], pb[h][:, :],
                            ms_sb[:, 8 + h:9 + h],
                            sg_sb[:, tau, h * 512:(h + 1) * 512],
                            A.mult, A.mult),
                            waits=[("se", srf), st["dsOG"],
                                   st.get("dsOG1")], inc="ve",
                            rd=["pb%d" % h, "msC", "sg"], wr=["og"])
                    st["dsOG1"] = P.dma("ds", "sync", lambda e:
                                        e.dma_start_transpose(
                        _ap(og_fm, 0, [[16 * 128, 128], [128, 8], [1, 128]]),
                        og_tm[:, 0:1024]),
                        waits=[("ve", vog), ("pe", st["peY"])])
                    for h in range(2, 4):
                        vog = P.op("vector", lambda e, h=h, tau=tau:
                                   e.scalar_tensor_tensor(
                            og_tm[:, h * 512:(h + 1) * 512], pb[h][:, :],
                            ms_sb[:, 8 + h:9 + h],
                            sg_sb[:, tau, h * 512:(h + 1) * 512],
                            A.mult, A.mult),
                            waits=[("se", srf), st["dsOG"]], inc="ve",
                            rd=["pb%d" % h, "msC", "sg"], wr=["og"])
                    st["vOG"] = vog
                    st["dsOG"] = P.dma("ds", "sync", lambda e:
                                       e.dma_start_transpose(
                        _ap(og_fm, 8 * 128,
                            [[16 * 128, 128], [128, 8], [1, 128]]),
                        og_tm[:, 1024:2048]),
                        waits=[("ve", vog), ("pe", st["peY"])])
                    pey = None
                    for dvc in range(16):
                        for half in range(2):
                            bank = pb[4] if half == 0 else pb[7]
                            pey = P.op("tensor", lambda e, dvc=dvc, half=half,
                                       bank=bank: e.matmul(
                                bank[:, :], og_fm[:, dvc, :],
                                wo_sb[:, dvc, half * 512:(half + 1) * 512],
                                start=(dvc == 0), stop=(dvc == 15)),
                                waits=[st["dsOG1"] if dvc < 8 else st["dsOG"],
                                       ("ve", st["vX2"]),
                                       ("ve", st["vS"]),
                                       ("ve", wo_state["last"])],
                                inc="pe" if (dvc == 15 and half == 1)
                                else None)
                    st["peY"] = pey
                    if tau == 1 and sb + 1 < NSB:
                        for s in a_steps(sb + 1, 0):
                            s()
                    vx2 = None
                    for half in range(2):
                        bank = pb[4] if half == 0 else pb[7]
                        vx2 = P.op("vector", lambda e, half=half, bank=bank,
                                   tau=tau: e.scalar_tensor_tensor(
                            xbf[:, tau, half * 512:(half + 1) * 512],
                            bank[:, :],
                            1.0, xbf[:, tau, half * 512:(half + 1) * 512],
                            A.mult, A.add),
                            waits=[("pe", pey)], inc="ve",
                            rd=["pb%d" % (4 if half == 0 else 7), "xbf"],
                            wr=["xbf"])
                    st["vX2"] = vx2
                    st["vX2t%d" % tau] = vx2
                    st["dcX2t%d" % tau] = P.dma(
                        "dc", "scalar", lambda e, c=c, tau=tau: e.dma_start(
                            out=x2_d[c * 128:(c + 1) * 128, :],
                            in_=xbf[:, tau, :]),
                        waits=[("ve", vx2)])
                    if tau == 1 and sb + 1 < NSB:
                        ct2 = build_c(sb + 1)
                        for s in ct2[0][:7]:
                            s()
                        b2 = build_b(sb + 1)
                        b2[0]()
                        b2[3]()
                        nxt = {"ct": ct2, "pre": 7,
                               "b": [b2[1], b2[2], b2[4], b2[5]] + b2[6:]}
                    pump(1)

            while prep_left():
                pump(1)

            pro_sq = {}
            pro_rd = {}
            for tt in range(NT):
                off = (tt % 3) * 1024
                lp = lane_prev[off]
                dl = P.dma("dc", "scalar", lambda e, tt=tt, off=off:
                           e.dma_start(
                    out=qA_t[:, off:off + 1024],
                    in_=x2_d[tt * 128:(tt + 1) * 128, :]),
                    waits=[("ve", pro_rd.get(tt - 3, lp.get("v9", 0))),
                           ("se", pro_sq.get(tt - 3, lp.get("sG", 0))),
                           lp.get("dw0"), lp.get("dt")])
                pro_rd[tt] = P.op(
                    "vector", lambda e, tt=tt, off=off: e.tensor_reduce(
                        st2[:, tt:tt + 1], qA_t[:, off:off + 1024], AX, A.add),
                    waits=[dl], inc="ve", rd=["qA@%d" % off],
                    wr=["p2S%d" % tt])
                pro_sq[tt] = P.op(
                    "scalar", lambda e, tt=tt, off=off: e.activation(
                        scrA[:, :], qA_t[:, off:off + 1024], AF.Square,
                        accum_out=st2[:, 16 + tt:17 + tt]),
                    waits=[dl], inc="se", rd=["qA@%d" % off],
                    wr=["scrA", "p2Q%d" % tt])
            sk = ["p2S%d" % t for t in range(NT)]
            qk = ["p2Q%d" % t for t in range(NT)]
            P.op("vector", lambda e: e.tensor_scalar(
                st2[:, 0:NT], st2[:, 0:NT], 1.0 / H, None, A.mult),
                inc="ve", rd=sk, wr=sk)
            P.op("vector", lambda e: e.tensor_scalar(
                st2[:, 16:16 + NT], st2[:, 16:16 + NT], 1.0 / H, EPS,
                A.mult, A.add), waits=[("se", pro_sq[NT - 1])], inc="ve",
                rd=qk, wr=qk)
            P.op("vector", lambda e: e.scalar_tensor_tensor(
                st2[:, 32:32 + NT], st2[:, 0:NT], 1.0,
                st2[:, 0:NT], A.mult, A.mult), inc="ve", rd=sk, wr=["p2M"])
            vvar2 = P.op("vector", lambda e: e.tensor_tensor(
                st2[:, 16:16 + NT], st2[:, 16:16 + NT],
                st2[:, 32:32 + NT], A.subtract), inc="ve",
                rd=qk + ["p2M"], wr=qk)
            P.op("scalar", lambda e: e.activation(
                st2[:, 32:32 + NT], st2[:, 16:16 + NT], AF.Ln),
                waits=[("ve", vvar2)], inc="se", rd=qk, wr=["p2M"])
            srst = P.op("scalar", lambda e: e.activation(
                st2[:, 48:48 + NT], st2[:, 32:32 + NT], AF.Exp, scale=-0.5),
                inc="se", rd=["p2M"], wr=["p2R"])

        with contextlib.ExitStack() as fstack:
            EF = fstack.enter_context
            wgT_sb = EF(nc.sbuf_tensor("wgTs", [128, NIT, 8, 128], BF16))
            wuT_sb = EF(nc.sbuf_tensor("wuTs", [128, NIT, 8, 128], BF16))
            wdT_sb = EF(nc.sbuf_tensor("wdTs", [128, NIT, 8, 128], BF16))
            x2t = [EF(nc.sbuf_tensor("x2f%d" % i, [128, 1024], BF16))
                   for i in range(2)]
            xq_tm = EF(nc.sbuf_tensor("xqtm", [128, 1024], BF16))
            xq_fm = [EF(nc.sbuf_tensor("xqfm%d" % i, [128, 8, 128], BF16))
                     for i in range(2)]
            hq_tm = EF(nc.sbuf_tensor("hqtm", [128, 2816], BF16))
            hq_fm = EF(nc.sbuf_tensor("hqfm", [128, 22, 128], BF16))
            sgate = [EF(nc.sbuf_tensor("sgate%d" % i, [128, 512], F32))
                     for i in range(2)]
            out_t = EF(nc.sbuf_tensor("outt", [128, 1024], F32))

            barrier = ([("ve", P.v("ve")), ("pe", P.v("pe")), ("se", P.v("se"))]
                       + P.dma_fence("dc") + P.dma_fence("ds") + P.dma_fence("dg"))
            dwg = P.dma("ds", "sync", lambda e: e.dma_start(
                out=wgT_sb[:, :, :, :], in_=wgT_d[:, :, :, :]), waits=barrier)
            dwu = P.dma("ds", "sync", lambda e: e.dma_start(
                out=wuT_sb[:, :, :, :], in_=wuT_d[:, :, :, :]))
            dwd = P.dma("ds", "sync", lambda e: e.dma_start(
                out=wdT_sb[:, :, :, :], in_=wdT_d[:, :, :, :]))

            IGRP = [(0, 512), (512, 512), (1024, 512), (1536, 512), (2048, 512),
                    (2560, 256)]
            fst = {"peGU": {}, "seSi": {}, "vH": {}, "peD": 0, "vOut": 0,
                   "dsXQ": {}, "dsHQa": {}, "dsHQb": {}, "dsHQc": {},
                   "vOutN": {}, "dcY": {}, "xq": {}}

            def x_steps(tt):
                if tt >= NT:
                    return []
                buf = x2t[tt % 2]
                u = {}
                steps = [lambda: u.__setitem__("dl", P.dma(
                    "ds", "sync", lambda e: e.dma_start(
                        out=buf[:, :], in_=x2_d[tt * 128:(tt + 1) * 128, :]),
                    waits=[("ve", fst["vOutN"].get(tt - 2, 0)), ("se", srst)]))]
                steps.append(lambda: u.__setitem__("vln", P.op(
                    "vector", lambda e: e.tensor_scalar(
                        xq_tm[:, :], buf[:, :], st2[:, tt:tt + 1],
                        st2[:, 48 + tt:49 + tt], A.subtract, A.mult),
                    waits=[u["dl"], fst["dsXQ"].get(tt - 1)], inc="ve",
                    rd=["x2f%d" % (tt % 2), "p2S%d" % tt, "p2R"],
                    wr=["xqtm"])))
                qsteps, qm = quantize_steps(
                    0, 1024, (xq_tm, 0, 1024), (xq_tm, 0, 1024),
                    waits=lambda: [("ve", u["vln"])],
                    dst_waits=lambda: [fst["dsXQ"].get(tt - 1)])
                steps += qsteps
                fst["xq"][tt] = qm

                def dmat():
                    fst["dsXQ"][tt] = P.dma(
                        "ds", "sync", lambda e: e.dma_start_transpose(
                            xq_fm[tt % 2][:, :, :], xq_tm[:, :]),
                        waits=[("ve", qm["vq"]),
                               ("pe", fst["peGU"].get(tt - 2, 0)), dwg, dwu])
                steps.append(dmat)
                return steps

            def h_steps(tt):
                buf = x2t[tt % 2]
                steps = []
                for gi, (i0, w) in enumerate(IGRP):
                    pg = pb[2 * (gi % 2)]
                    pu = pb[2 * (gi % 2) + 1]
                    it0 = i0 // 128
                    nt_ = w // 128

                    def mm(gi=gi, pg=pg, pu=pu, it0=it0, nt_=nt_, w=w):
                        pegu = None
                        for hc in range(8):
                            movg = _ap(wgT_sb, it0 * 8 * 128 + hc * 128,
                                       [[NIT * 8 * 128, 128], [8 * 128, nt_],
                                        [1, 128]])
                            movu = _ap(wuT_sb, it0 * 8 * 128 + hc * 128,
                                       [[NIT * 8 * 128, 128], [8 * 128, nt_],
                                        [1, 128]])
                            P.op("tensor", lambda e, hc=hc, movg=movg: e.matmul(
                                pg[:, 0:w], xq_fm[tt % 2][:, hc, :], movg,
                                start=(hc == 0), stop=(hc == 7)),
                                waits=[fst["dsXQ"][tt],
                                       ("se", fst["seSi"].get((tt, gi - 2),
                                        fst["seSi"].get((tt - 1, 5), 0))),
                                       ("ve", fst["vH"].get((tt, gi - 2),
                                        fst["vH"].get((tt - 1, 5), 0)))],
                                inc=None)
                            pegu = P.op("tensor", lambda e, hc=hc, movu=movu:
                                        e.matmul(
                                pu[:, 0:w], xq_fm[tt % 2][:, hc, :], movu,
                                start=(hc == 0), stop=(hc == 7)),
                                inc="pe" if hc == 7 else None)
                        fst["peGU"][tt] = pegu

                    def silu(gi=gi, pg=pg, w=w):
                        fst["seSi"][(tt, gi)] = P.op(
                            "scalar", lambda e: e.activation(
                                sgate[gi % 2][:, 0:w], pg[:, 0:w], AF.Silu),
                            waits=[("pe", fst["peGU"][tt]),
                                   ("ve", fst["vH"].get((tt, gi - 2),
                                    fst["vH"].get((tt - 1, 5), 0)))],
                            inc="se", rd=["pb%d" % (2 * (gi % 2))],
                            wr=["sgate%d" % (gi % 2)])

                    def vh(gi=gi, pu=pu, w=w, i0=i0):
                        fst["vH"][(tt, gi)] = P.op(
                            "vector", lambda e: e.tensor_tensor(
                                hq_tm[:, i0:i0 + w], sgate[gi % 2][:, 0:w],
                                pu[:, 0:w], A.mult),
                            waits=[("se", fst["seSi"][(tt, gi)]),
                                   fst["dsHQ" + "abc"[gi // 2]].get(tt - 1),
                                   ("se", fst.get("seQ2", 0))], inc="ve",
                            rd=["sgate%d" % (gi % 2),
                                "pb%d" % (2 * (gi % 2) + 1)],
                            wr=["hqtm"])

                    steps += [mm, silu, vh]
                qa_s, m_a = quantize_steps(
                    1024, 1024, (hq_tm, 0, 2816), (hq_tm, 0, 2816),
                    waits=lambda: [("ve", fst["vH"][(tt, 1)])],
                    dst_waits=())
                qb_s, m_b = quantize_steps(
                    2048, 1024, (hq_tm, 1024, 2816), (hq_tm, 1024, 2816),
                    waits=lambda: [("ve", fst["vH"][(tt, 3)])],
                    dst_waits=())
                qc_s, m_c = quantize_steps(
                    1024, 768, (hq_tm, 2048, 2816), (hq_tm, 2048, 2816),
                    waits=lambda: [("ve", fst["vH"][(tt, 5)])],
                    dst_waits=())
                u = {}
                grp = steps
                steps = grp[:6]

                def zip_into(dst, big, small, ratio):
                    b = s = 0
                    while b < len(big) or s < len(small):
                        for _ in range(ratio):
                            if b < len(big):
                                dst.append(big[b])
                                b += 1
                        if s < len(small):
                            dst.append(small[s])
                            s += 1

                zip_into(steps, qa_s[:8], grp[6:12], 1)
                ab = []
                ai = bi2 = 0
                qa_rest = qa_s[8:]
                while ai < len(qa_rest) or bi2 < len(qb_s):
                    if ai < len(qa_rest):
                        ab.append(qa_rest[ai])
                        ai += 1
                    if bi2 < len(qb_s):
                        ab.append(qb_s[bi2])
                        bi2 += 1
                zip_into(steps, ab, grp[12:18], 6)

                def dmat_a():
                    u["da"] = P.dma(
                        "ds", "sync", lambda e: e.dma_start_transpose(
                            _ap(hq_fm, 0, [[22 * 128, 128], [128, 8], [1, 128]]),
                            hq_tm[:, 0:1024]),
                        waits=[("ve", m_a["vq"]), ("pe", fst["peD"]), dwd])
                    fst["dsHQa"][tt] = u["da"]
                steps.append(dmat_a)

                def dn_a():
                    ped = None
                    for ic in range(8):
                        for half in range(2):
                            bank = pb[4 + half]
                            mov = _ap(wdT_sb, ic * 8 * 128 + half * 4 * 128,
                                      [[NIT * 8 * 128, 128], [1, 512]])
                            ped = P.op("tensor", lambda e, bank=bank, mov=mov,
                                       ic=ic: e.matmul(
                                bank[:, :], hq_fm[:, ic, :], mov,
                                start=(ic == 0), stop=False),
                                waits=[u["da"], ("ve", fst["vOut"])],
                                inc="pe" if (ic == 7 and half == 1) else None)
                    u["pa"] = ped
                steps.append(dn_a)

                def dmat_b():
                    u["db"] = P.dma(
                        "ds", "sync", lambda e: e.dma_start_transpose(
                            _ap(hq_fm, 8 * 128,
                                [[22 * 128, 128], [128, 8], [1, 128]]),
                            hq_tm[:, 1024:2048]),
                        waits=[("ve", m_b["vq"]), ("pe", fst["peD"])])
                    fst["dsHQb"][tt] = u["db"]
                steps.append(dmat_b)

                def dn_b():
                    ped = None
                    for ic in range(8, 16):
                        for half in range(2):
                            bank = pb[4 + half]
                            mov = _ap(wdT_sb, ic * 8 * 128 + half * 4 * 128,
                                      [[NIT * 8 * 128, 128], [1, 512]])
                            ped = P.op("tensor", lambda e, bank=bank, mov=mov,
                                       ic=ic: e.matmul(
                                bank[:, :], hq_fm[:, ic, :], mov,
                                start=False, stop=False),
                                waits=[u["db"]],
                                inc="pe" if (ic == 15 and half == 1) else None)
                    u["pb"] = ped
                steps.append(dn_b)
                steps += qc_s
                steps.append(lambda: fst.__setitem__("seQ2", m_c["s7"]))

                def dmat_c():
                    u["dc"] = P.dma(
                        "ds", "sync", lambda e: e.dma_start_transpose(
                            _ap(hq_fm, 16 * 128,
                                [[22 * 128, 128], [128, 6], [1, 128]]),
                            hq_tm[:, 2048:2816]),
                        waits=[("ve", m_c["vq"]), ("pe", fst["peD"])])
                    fst["dsHQc"][tt] = u["dc"]
                steps.append(dmat_c)

                def dn_c():
                    ped = None
                    for ic in range(16, NIT):
                        for half in range(2):
                            bank = pb[4 + half]
                            mov = _ap(wdT_sb, ic * 8 * 128 + half * 4 * 128,
                                      [[NIT * 8 * 128, 128], [1, 512]])
                            ped = P.op("tensor", lambda e, bank=bank, mov=mov,
                                       ic=ic: e.matmul(
                                bank[:, :], hq_fm[:, ic, :], mov,
                                start=False, stop=(ic == NIT - 1)),
                                waits=[u["dc"]],
                                inc="pe" if (ic == NIT - 1 and half == 1)
                                else None)
                    fst["peD"] = ped
                steps.append(dn_c)

                def vout():
                    vo = None
                    for half in range(2):
                        bank = pb[4 + half]
                        vo = P.op("vector", lambda e, half=half, bank=bank:
                                  e.scalar_tensor_tensor(
                            out_t[:, half * 512:(half + 1) * 512], bank[:, :],
                            1.0, buf[:, half * 512:(half + 1) * 512],
                            A.mult, A.add),
                            waits=[("pe", fst["peD"]),
                                   fst["dcY"].get(tt - 1)],
                            inc="ve", rd=["pb%d" % (4 + half),
                                          "x2f%d" % (tt % 2)],
                            wr=["outt"])
                    fst["vOut"] = vo
                    fst["vOutN"][tt] = vo
                    fst["dcY"][tt] = P.dma("dc", "scalar", lambda e:
                                           e.dma_start(
                        out=y_d[tt * 128:(tt + 1) * 128, :], in_=out_t[:, :]),
                        waits=[("ve", vo)])
                steps.append(vout)
                return steps

            def zip_emit(big, small, ratio=3):
                bi, si = list(big), list(small)
                b = s = 0
                while b < len(bi) or s < len(si):
                    for _ in range(ratio):
                        if b < len(bi):
                            bi[b]()
                            b += 1
                    if s < len(si):
                        si[s]()
                        s += 1

            for s in x_steps(0):
                s()
            for tt in range(NT):
                nxt = x_steps(tt + 1)
                head, rest = nxt[:7], nxt[7:]
                for s in head:
                    s()
                zip_emit(h_steps(tt), rest, 2)

            P.op("gpsimd", lambda e: e.engine_nop(),
                 waits=P.dma_fence("dc") + P.dma_fence("ds") + P.dma_fence("dg"),
                 drain=False)

        with nc.Block() as block:
            @block.tensor
            def _(e):
                for fn in P.ops["tensor"]:
                    fn(e)

            @block.vector
            def _(e):
                for fn in P.ops["vector"]:
                    fn(e)

            @block.scalar
            def _(e):
                for fn in P.ops["scalar"]:
                    fn(e)

            @block.gpsimd
            def _(e):
                for fn in P.ops["gpsimd"]:
                    fn(e)

            @block.sync
            def _(e):
                for fn in P.ops["sync"]:
                    fn(e)

    return nc


def make_consts():
    mask = np.triu(np.ones((128, 128), dtype=np.float32))
    return np.tile(mask, (1, 4)).astype(ml_dtypes.bfloat16)


def kernel(L0_attn_hidden_states, L0_attn_Wq, L0_attn_Wk, L0_attn_Wv, L0_attn_Wg,
           L0_attn_Wgk1, L0_attn_Wgk2, L0_attn_bgk2, L0_attn_g_norm_w, L0_attn_Wo,
           L0_ffn_w_gate, L0_ffn_w_up, L0_ffn_w_down):
    hs = np.asarray(L0_attn_hidden_states)
    B, T, _ = hs.shape
    nc = build_program(T)
    bgk2l = np.ascontiguousarray(
        np.asarray(L0_attn_bgk2, dtype=np.float32).reshape(8, 128).T)
    gnwl = np.ascontiguousarray(
        np.tile(np.asarray(L0_attn_g_norm_w, dtype=np.float32), 4).reshape(16, 128).T)
    maskT = make_consts()
    common = {
        "Wq": np.ascontiguousarray(L0_attn_Wq),
        "Wk": np.ascontiguousarray(L0_attn_Wk),
        "Wv": np.ascontiguousarray(L0_attn_Wv),
        "Wg": np.ascontiguousarray(L0_attn_Wg),
        "Wgk1": np.ascontiguousarray(L0_attn_Wgk1),
        "Wgk2": np.ascontiguousarray(L0_attn_Wgk2),
        "bgk2l": bgk2l, "gnwl": gnwl,
        "Wo": np.ascontiguousarray(L0_attn_Wo),
        "w_gate": np.ascontiguousarray(L0_ffn_w_gate),
        "w_up": np.ascontiguousarray(L0_ffn_w_up),
        "w_down": np.ascontiguousarray(L0_ffn_w_down),
        "maskT": maskT,
    }
    in_maps = [dict(common, x=np.ascontiguousarray(hs[i])) for i in range(B)]
    res = run_bass_kernel_spmd(nc, in_maps, core_ids=list(range(B)))
    out = np.stack([np.asarray(res.results[i]["y"]) for i in range(B)], axis=0)
    return out.astype(np.float32)
